# version 01 (final)
# baseline (speedup 1.0000x reference)
import numpy as np
import ml_dtypes
from contextlib import ExitStack
import concourse.bass as bass
import concourse.mybir as mybir
from concourse.bass_utils import run_bass_kernel_spmd
from concourse.replica_groups import maybe_share_collective_output_space

F32 = mybir.dt.float32
BF16 = mybir.dt.bfloat16
AF = mybir.ActivationFunctionType
ALU = mybir.AluOpType
AX = mybir.AxisListType

MAGIC = float(np.float32(1.5 * 2**23))
STAGE = 6
DEBUG = False

T, C, D, F3, FF = 1024, 4096, 1024, 3072, 16384
TC, CCn, F3C, FCn = 8, 32, 24, 128
FCL = 16
G8 = [list(range(8))]
G4 = [[0, 1, 2, 3], [4, 5, 6, 7]]
EPS = 1e-5
THR = 3.0
INV127 = 1.0 / 127.0
SMIN = 1e-8
ISQ = 1.0 / 32.0


class Ctr:
    def __init__(self, nc, es, name):
        self.sem = es.enter_context(nc.semaphore(name))
        self.v = 0

    def dma(self, inst):
        inst.then_inc(self.sem, 16)
        self.v += 16
        return self.v

    def cc(self, inst):
        inst.then_inc(self.sem)
        self.v += 1
        return self.v

    def inc(self, inst):
        inst.then_inc(self.sem, 1)
        self.v += 1
        return self.v


def build_program(stage=STAGE, debug=DEBUG):
    nc = bass.Bass("TRN2", target_bir_lowering=False)
    es = ExitStack()
    dram = nc.dram_tensor

    x_my = dram("x_my", [T, C], F32, kind="ExternalInput")
    wqkv = dram("wqkv", [F3C, 128, C], F32, kind="ExternalInput")
    wo = dram("wo", [D, C], F32, kind="ExternalInput")
    w1full = dram("w1full", [FCn * 128, C], F32, kind="ExternalInput")
    w2full = dram("w2full", [FF, C], F32, kind="ExternalInput")
    w2scan = dram("w2scan", [FCL * 128, C], F32, kind="ExternalInput")
    bqkv_col = dram("bqkv_col", [128, F3C], F32, kind="ExternalInput")
    b1_col = dram("b1_col", [128, FCn], F32, kind="ExternalInput")
    bo_row = dram("bo_row", [1, C], F32, kind="ExternalInput")
    b2_row = dram("b2_row", [1, C], F32, kind="ExternalInput")
    identbf_d = dram("identbf_d", [128, 128], BF16, kind="ExternalInput")
    identf_d = dram("identf_d", [128, 128], F32, kind="ExternalInput")
    out_my = dram("out_my", [T, C], F32, kind="ExternalOutput")

    xn_sp = dram("xn_sp", [T, C], F32)
    h_d = dram("h_d", [FF, T], F32)
    hq_d = dram("hq_d", [FF, T], BF16)
    ho_d = dram("ho_d", [FF, T], BF16)
    attn_sp = dram("attn_sp", [T, C], F32)
    bnc = dram("bnc", [1, FF], F32)
    xnbf_snap = dram("xnbf_snap", [128, CCn * T], BF16)
    xnbf_snap2 = dram("xnbf_snap2", [128, CCn * T], BF16)

    def ccout(name, shape, dt, kind, groups):
        return dram(name, shape, dt,
                    addr_space=maybe_share_collective_output_space(kind, groups))
    ar0_in = dram("ar0_in", [128, CCn], F32)
    ar0_out = ccout("ar0_out", [128, CCn], F32, "AllReduce", G8)
    ar1_in = dram("ar1_in", [128, CCn], F32)
    ar1_out = ccout("ar1_out", [128, CCn], F32, "AllReduce", G8)
    ar3_in = dram("ar3_in", [128, FCn], F32)
    ar3_out = ccout("ar3_out", [128, FCn], F32, "AllReduce", G8)

    kv_in = dram("kv_in", [2048, D], BF16)
    kv_out = ccout("kv_out", [4 * 2048, D], BF16, "AllGather", G4)
    kvf_out = ccout("kvf_out", [4 * 64, D], BF16, "AllGather", G4)

    dbg = {}
    if debug:
        for name, shape, dt in [
            ("d_xn", [T, C], F32), ("d_cm1", [128, CCn], F32),
            ("d_q", [128, 8 * T], BF16), ("d_kv", [4 * 2048, D], BF16),
            ("d_h", [FF, T], F32), ("d_cmh", [128, FCn], F32),
            ("d_attn", [T, C], F32), ("d_rm2", [128, TC], F32),
            ("d_hq", [FF, T], BF16), ("d_snap", [128, CCn * T], BF16),
            ("d_w1ag", [2 * FCL * 128, C], BF16),
            ("d_w2ag", [2 * FCL * 128, C], BF16), ("d_xq", [128, CCn * T], BF16),
        ]:
            dbg[name] = dram(name, shape, dt, kind="ExternalOutput")

    sQ = Ctr(nc, es, "sQ")
    sG = Ctr(nc, es, "sG")
    sD = Ctr(nc, es, "sD")
    sV = Ctr(nc, es, "sV")
    sA = Ctr(nc, es, "sA")
    sT = Ctr(nc, es, "sT")
    sCC = Ctr(nc, es, "sCC")
    sR = Ctr(nc, es, "sR")
    sGR = Ctr(nc, es, "sGR")
    sFen = Ctr(nc, es, "sFen")
    sKVF = Ctr(nc, es, "sKVF")

    def S(stack, name, shape, dt):
        return stack.enter_context(nc.sbuf_tensor(name, shape, dt))

    def P(stack, name, shape, dt):
        return stack.enter_context(nc.psum_tensor(name, shape, dt))

    ident_bf = S(es, "ident_bf", [128, 128], BF16)
    ident_f = S(es, "ident_f", [128, 128], F32)
    ones_row = S(es, "ones_row", [1, 128], F32)
    bqkv_c = S(es, "bqkv_c", [128, F3C], F32)
    b1_c = S(es, "b1_c", [128, FCn], F32)
    mask1_c = S(es, "mask1_c", [128, CCn], F32)
    mi1_c = S(es, "mi1_c", [128, CCn], F32)
    mask2_c = S(es, "mask2_c", [128, FCn], F32)
    mi2_c = S(es, "mi2_c", [128, FCn], F32)
    cm1loc = S(es, "cm1loc", [128, CCn], F32)
    sw1cols = S(es, "sw1cols", [128, FCL], F32)
    sw2loc = S(es, "sw2loc", [128, CCn], F32)
    cmh = S(es, "cmh", [128, FCn], F32)
    rm2c = S(es, "rm2c", [128, TC], F32)
    swr16 = S(es, "swr16", [32, 128], F32)

    block = es.enter_context(nc.Block())

    def snap():
        return {"V": sV.v, "A": sA.v, "T": sT.v, "Q": sQ.v, "G": sG.v, "D": sD.v}

    def bar_waits(eng, sn, which="VAT"):
        for c, ctr in (("V", sV), ("A", sA), ("T", sT),
                       ("Q", sQ), ("G", sG), ("D", sD)):
            if c in which and sn[c]:
                eng.wait_ge(ctr.sem, sn[c])

    def vsync(vector, inst):
        sV.inc(inst)
        vector.wait_ge(sV.sem, sV.v)

    def asyncr(scalar, inst):
        sA.inc(inst)
        scalar.wait_ge(sA.sem, sA.v)

    def bcastp(src_ap, dst, n, ps, mm_list, ev_list, first_waits=()):
        for i in range(0, n, 512):
            w = min(512, n - i)

            @block.tensor
            def _(tensor, i=i, w=w):
                if i == 0:
                    for sem, val in first_waits:
                        tensor.wait_ge(sem, val)
                if ev_list:
                    tensor.wait_ge(sV.sem, ev_list[-1])
                mm_list.append(sT.inc(tensor.matmul(
                    ps[:, 0:w], ones_row[:, :], src_ap[0:1, i:i + w],
                    start=True, stop=True)))

            @block.vector
            def _(vector, i=i, w=w):
                vector.wait_ge(sT.sem, mm_list[-1])
                ev_list.append(sV.inc(vector.tensor_copy(dst[:, i:i + w],
                                                         ps[:, 0:w])))

    @block.sync
    def _(sync):
        sQ.dma(sync.dma_start(out=ident_bf[:], in_=identbf_d[:, :]))
        sQ.dma(sync.dma_start(out=ident_f[:], in_=identf_d[:, :]))
        sQ.dma(sync.dma_start(out=bqkv_c[:], in_=bqkv_col[:, :]))
        sQ.dma(sync.dma_start(out=b1_c[:], in_=b1_col[:, :]))
    q_consts = sQ.v

    @block.vector
    def _(vector):
        vector.wait_ge(sQ.sem, q_consts)
        sV.inc(vector.memset(ones_row[:], 1.0))

    p1 = ExitStack()
    xt = [S(p1, f"p1x{i}", [128, C], F32) for i in range(2)]
    xnt = [S(p1, f"p1n{i}", [128, C], F32) for i in range(2)]
    sqt = S(p1, "p1sq", [128, C], F32)
    cmacc = S(p1, "p1cm", [128, C], F32)
    cst = S(p1, "p1c", [128, 12], F32)
    ptf = [P(p1, f"p1pf{i}", [128, 512], F32) for i in range(2)]

    p1_ld, p1_sq, p1_stats, p1_sqrt, p1_rstd = [], [], [], [], []
    p1_xn, p1_sp, p1_cm = [], [], []

    @block.vector
    def _(vector):
        sV.inc(vector.memset(cmacc[:], 0.0))

    for i in range(TC):
        @block.sync
        def _(sync, i=i):
            if i >= 2:
                sync.wait_ge(sA.sem, p1_xn[i - 2])
            p1_ld.append(sQ.dma(sync.dma_start(
                out=xt[i % 2][:], in_=x_my[i * 128:(i + 1) * 128, :])))

        @block.scalar
        def _(scalar, i=i):
            scalar.wait_ge(sQ.sem, p1_ld[i])
            p1_sq.append(sA.inc(scalar.activation(
                sqt[:], xt[i % 2][:], AF.Square, accum_out=cst[:, 1:2])))

        @block.vector
        def _(vector, i=i):
            vector.wait_ge(sQ.sem, p1_ld[i])
            vsync(vector, vector.reduce_sum(cst[:, 0:1], xt[i % 2][:], AX.X))
            vsync(vector, vector.tensor_scalar(cst[:, 2:3], cst[:, 0:1],
                                               1.0 / C, None, ALU.mult))
            vsync(vector, vector.tensor_mul(cst[:, 4:5], cst[:, 2:3],
                                            cst[:, 2:3]))
            vector.wait_ge(sA.sem, p1_sq[i])
            vsync(vector, vector.tensor_scalar(cst[:, 3:4], cst[:, 1:2],
                                               1.0 / C, None, ALU.mult))
            p1_stats.append(sV.inc(vector.tensor_scalar(
                cst[:, 6:7], cst[:, 3:4], cst[:, 4:5], EPS,
                ALU.subtract, ALU.add)))

        @block.scalar
        def _(scalar, i=i):
            scalar.wait_ge(sV.sem, p1_stats[i])
            p1_sqrt.append(sA.inc(scalar.activation(
                cst[:, 7:8], cst[:, 6:7], AF.Sqrt)))

        @block.vector
        def _(vector, i=i):
            vector.wait_ge(sA.sem, p1_sqrt[i])
            vsync(vector, vector.reciprocal(cst[:, 8:9], cst[:, 7:8]))
            p1_rstd.append(sV.inc(vector.scalar_tensor_tensor(
                cst[:, 9:10], cst[:, 2:3], -1.0, cst[:, 8:9], ALU.mult, ALU.mult)))

        @block.scalar
        def _(scalar, i=i):
            scalar.wait_ge(sV.sem, p1_rstd[i])
            if i >= 2:
                scalar.wait_ge(sD.sem, p1_sp[i - 2])
                scalar.wait_ge(sV.sem, p1_cm[i - 2])
            p1_xn.append(sA.inc(scalar.activation(
                xnt[i % 2][:], xt[i % 2][:], AF.Identity,
                bias=cst[:, 9:10], scale=cst[:, 8:9])))
            scalar.wait_ge(sA.sem, p1_xn[i])
            p1_sp.append(sD.dma(scalar.dma_start(
                out=xn_sp[i * 128:(i + 1) * 128, :], in_=xnt[i % 2][:])))

        @block.vector
        def _(vector, i=i):
            vector.wait_ge(sA.sem, p1_xn[i])
            vector.tensor_tensor(cmacc[:], cmacc[:], xnt[i % 2][:], ALU.max)
            p1_cm.append(sV.inc(vector.scalar_tensor_tensor(
                cmacc[:], xnt[i % 2][:], -1.0, cmacc[:], ALU.mult, ALU.max)))

    p1b_tr, p1b_rd = [], []
    for r in range(CCn):
        @block.tensor
        def _(tensor, r=r):
            if r == 0:
                tensor.wait_ge(sV.sem, p1_cm[-1])
            if r >= 2:
                tensor.wait_ge(sV.sem, p1b_rd[r - 2])
            p1b_tr.append(sT.inc(tensor.transpose(
                ptf[r % 2][:, 0:128], cmacc[:, r * 128:(r + 1) * 128], ident_f[:])))

        @block.vector
        def _(vector, r=r):
            vector.wait_ge(sT.sem, p1b_tr[r])
            p1b_rd.append(sV.inc(vector.reduce_max(
                cm1loc[:, r:r + 1], ptf[r % 2][:, 0:128], AX.X)))

    @block.gpsimd
    def _(gpsimd):
        gpsimd.wait_ge(sV.sem, p1b_rd[-1])
        sG.dma(gpsimd.dma_start(out=ar1_in[:, :], in_=cm1loc[:]))
        gpsimd.wait_ge(sG.sem, sG.v)
        sCC.cc(gpsimd.collective_compute(
            "AllReduce", ALU.max, replica_groups=G8,
            ins=[ar1_in[:, :]], outs=[ar1_out[:, :]]))
    cc_ar1 = sCC.v

    p1_end = snap()
    p1.close()

    pw = ExitStack()
    w2b = [S(pw, f"wb{i}", [128, C], F32) for i in range(2)]
    acc2s = S(pw, "acc2s", [128, C], F32)
    ptf2 = [P(pw, f"pwf{i}", [128, 512], F32) for i in range(2)]

    p2_ld, p2_ac = [], []

    @block.vector
    def _(vector):
        bar_waits(vector, p1_end, "A")
        sV.inc(vector.memset(acc2s[:], 0.0))

    for j in range(FCL):
        @block.sync
        def _(sync, j=j):
            if j == 0:
                bar_waits(sync, p1_end, "VATD")
            if j >= 2:
                sync.wait_ge(sV.sem, p2_ac[j - 2])
            p2_ld.append(sQ.dma(sync.dma_start(
                out=w2b[j % 2][:], in_=w2scan[j * 128:(j + 1) * 128, :])))

        @block.vector
        def _(vector, j=j):
            vector.wait_ge(sQ.sem, p2_ld[j])
            vector.tensor_tensor(acc2s[:], acc2s[:], w2b[j % 2][:], ALU.max)
            p2_ac.append(sV.inc(vector.scalar_tensor_tensor(
                acc2s[:], w2b[j % 2][:], -1.0, acc2s[:], ALU.mult, ALU.max)))

    p2b_tr, p2b_rd = [], []
    for r in range(CCn):
        @block.tensor
        def _(tensor, r=r):
            if r == 0:
                tensor.wait_ge(sV.sem, p2_ac[-1])
                bar_waits(tensor, p1_end, "V")
            if r >= 2:
                tensor.wait_ge(sV.sem, p2b_rd[r - 2])
            p2b_tr.append(sT.inc(tensor.transpose(
                ptf2[r % 2][:, 0:128], acc2s[:, r * 128:(r + 1) * 128], ident_f[:])))

        @block.vector
        def _(vector, r=r):
            vector.wait_ge(sT.sem, p2b_tr[r])
            p2b_rd.append(sV.inc(vector.reduce_max(
                sw2loc[:, r:r + 1], ptf2[r % 2][:, 0:128], AX.X)))

    @block.gpsimd
    def _(gpsimd):
        gpsimd.wait_ge(sV.sem, p2b_rd[-1])
        sG.dma(gpsimd.dma_start(out=ar0_in[:, :], in_=sw2loc[:]))
        gpsimd.wait_ge(sG.sem, sG.v)
        sCC.cc(gpsimd.collective_compute(
            "AllReduce", ALU.max, replica_groups=G8,
            ins=[ar0_in[:, :]], outs=[ar0_out[:, :]]))
    cc_ar0 = sCC.v

    @block.gpsimd
    def _(gpsimd):
        gpsimd.wait_ge(sCC.sem, cc_ar1)
        sG.dma(gpsimd.dma_start(out=cm1loc[:], in_=ar1_out[:, :]))
    g_cm1 = sG.v

    @block.vector
    def _(vector):
        vector.wait_ge(sG.sem, g_cm1)
        vsync(vector, vector.tensor_scalar(mask1_c[:], cm1loc[:], THR, None,
                                           ALU.is_gt))
        sV.inc(vector.tensor_scalar(mi1_c[:], mask1_c[:], -1.0, 1.0,
                                    ALU.mult, ALU.add))
    v_mask1 = sV.v

    pw_end = snap()
    pw.close()

    if stage < 2:
        return _finish(nc, block, dbg, debug, locals())

    s_q = ExitStack()
    q_sb = s_q.enter_context(nc.sbuf_tensor("q_sb", [128, 8 * T], BF16,
                                            side="right"))
    s_act = ExitStack()
    xnbf_fm = S(s_act, "xnbf_fm", [128, CCn * T], BF16)
    xq_fm = S(s_act, "xq_fm", [128, CCn * T], BF16)

    p65 = ExitStack()
    xr65 = [S(p65, f"x65_{i}", [128, C], F32) for i in range(2)]
    xb65 = [S(p65, f"b65_{i}", [128, C], BF16) for i in range(2)]
    ptb = [P(p65, f"ptb{i}", [128, 1024], BF16) for i in range(4)]

    p65_ld, p65_c, p65_tr, p65_ev = [], [], [], []
    for i in range(TC):
        @block.sync
        def _(sync, i=i):
            if i == 0:
                bar_waits(sync, pw_end, "VATG")
                sync.wait_ge(sD.sem, p1_sp[-1])
            if i >= 2:
                sync.wait_ge(sA.sem, p65_c[i - 2])
            p65_ld.append(sQ.dma(sync.dma_start(
                out=xr65[i % 2][:], in_=xn_sp[i * 128:(i + 1) * 128, :])))

        @block.scalar
        def _(scalar, i=i):
            scalar.wait_ge(sQ.sem, p65_ld[i])
            if i >= 2:
                scalar.wait_ge(sV.sem, p65_ev[(i - 2) * CCn + CCn - 1])
            p65_c.append(sA.inc(scalar.activation(
                xb65[i % 2][:], xr65[i % 2][:], AF.Identity)))

        for cc in range(CCn):
            @block.tensor
            def _(tensor, i=i, cc=cc):
                n = i * CCn + cc
                if cc == 0:
                    tensor.wait_ge(sA.sem, p65_c[i])
                    if i == 0:
                        bar_waits(tensor, pw_end, "V")
                if n >= 1:
                    tensor.wait_ge(sV.sem, p65_ev[n - 1])
                p65_tr.append(sT.inc(tensor.transpose(
                    ptb[n % 4][:, 0:128],
                    xb65[i % 2][:, cc * 128:(cc + 1) * 128], ident_bf[:])))

            @block.vector
            def _(vector, i=i, cc=cc):
                n = i * CCn + cc
                vector.wait_ge(sT.sem, p65_tr[n])
                p65_ev.append(sV.inc(vector.tensor_copy(
                    xnbf_fm[:, cc * T + i * 128:cc * T + (i + 1) * 128],
                    ptb[n % 4][:, 0:128])))

    @block.gpsimd
    def _(gpsimd):
        gpsimd.wait_ge(sV.sem, p65_ev[-1])
        sG.dma(gpsimd.dma_start(out=xnbf_snap[:, :], in_=xnbf_fm[:]))

    p65_end = snap()
    p65.close()

    if stage < 3:
        return _finish(nc, block, dbg, debug, locals())

    p75 = ExitStack()
    xr75 = S(p75, "x75", [128, C], F32)
    xqb = [S(p75, f"xqb{i}", [128, C], BF16) for i in range(2)]
    mi1b = S(p75, "mi1b", [128, C], BF16)
    row75 = S(p75, "row75", [1, C], F32)
    c75 = S(p75, "c75", [128, 4], F32)
    ptq = [P(p75, f"ptq{i}", [128, 1024], BF16) for i in range(4)]
    pbc2 = P(p75, "pbc2", [128, 512], F32)

    @block.tensor
    def _(tensor):
        bar_waits(tensor, p65_end, "VA")
        tensor.wait_ge(sV.sem, v_mask1)
        sT.inc(tensor.transpose(pbc2[0:CCn, 0:128], mi1_c[:], ident_f[:]))
    t_mi1t = sT.v

    @block.vector
    def _(vector):
        vector.wait_ge(sT.sem, t_mi1t)
        sV.inc(vector.tensor_copy(swr16[0:CCn, 0:128], pbc2[0:CCn, 0:128]))
    v_mi1t = sV.v

    @block.gpsimd
    def _(gpsimd):
        gpsimd.wait_ge(sV.sem, v_mi1t)
        sGR.dma(gpsimd.dma_start(
            out=bnc[0:1, 0:C].rearrange("a (r c) -> (a r) c", r=CCn, c=128),
            in_=swr16[0:CCn, 0:128]))
    g_mi1b = sGR.v

    @block.sync
    def _(sync):
        sync.wait_ge(sGR.sem, g_mi1b)
        sR.dma(sync.dma_start(out=row75[0:1, 0:C], in_=bnc[0:1, 0:C]))
    q_mi1r = sR.v

    bc2_mm, bc2_ev = [], []
    bcastp(row75, mi1b, C, pbc2, bc2_mm, bc2_ev,
           first_waits=((sR.sem, q_mi1r),))

    p75_ld, p75_q, p75_tr, p75_ev = [], [], [], []
    for i in range(TC):
        @block.sync
        def _(sync, i=i):
            if i == 0:
                sync.wait_ge(sV.sem, bc2_ev[-1])
            if i >= 1:
                sync.wait_ge(sV.sem, p75_q[i - 1])
            p75_ld.append(sQ.dma(sync.dma_start(
                out=xr75[:], in_=xn_sp[i * 128:(i + 1) * 128, :])))

        @block.vector
        def _(vector, i=i):
            vector.wait_ge(sQ.sem, p75_ld[i])
            if i >= 2:
                vector.wait_ge(sV.sem, p75_ev[(i - 2) * CCn + CCn - 1])
            vector.tensor_tensor(xr75[:], xr75[:], mi1b[:], ALU.mult)
            vsync(vector, vector.reduce_max(c75[:, 0:1], xr75[:], AX.X,
                                            apply_absolute_value=True))
            vsync(vector, vector.tensor_scalar(c75[:, 1:2], c75[:, 0:1],
                                               INV127, SMIN, ALU.mult, ALU.max))
            vsync(vector, vector.reciprocal(c75[:, 2:3], c75[:, 1:2]))
            vector.tensor_scalar(xr75[:], xr75[:], c75[:, 2:3], MAGIC,
                                 ALU.mult, ALU.add)
            p75_q.append(sV.inc(vector.tensor_scalar(
                xqb[i % 2][:], xr75[:], MAGIC, c75[:, 1:2],
                ALU.subtract, ALU.mult)))

        for cc in range(CCn):
            @block.tensor
            def _(tensor, i=i, cc=cc):
                n = i * CCn + cc
                if cc == 0:
                    tensor.wait_ge(sV.sem, p75_q[i])
                if n >= 1:
                    tensor.wait_ge(sV.sem, p75_ev[n - 1])
                p75_tr.append(sT.inc(tensor.transpose(
                    ptq[n % 4][:, 0:128],
                    xqb[i % 2][:, cc * 128:(cc + 1) * 128], ident_bf[:])))

            @block.vector
            def _(vector, i=i, cc=cc):
                n = i * CCn + cc
                vector.wait_ge(sT.sem, p75_tr[n])
                p75_ev.append(sV.inc(vector.tensor_copy(
                    xq_fm[:, cc * T + i * 128:cc * T + (i + 1) * 128],
                    ptq[n % 4][:, 0:128])))

    @block.gpsimd
    def _(gpsimd):
        gpsimd.wait_ge(sV.sem, p75_ev[-1])
        sG.dma(gpsimd.dma_start(out=xnbf_snap2[:, :], in_=xnbf_fm[:]))

    p75_end = snap()
    p75.close()

    p7 = ExitStack()
    wq7 = [S(p7, f"wq7_{i}", [128, 2048], F32) for i in range(2)]
    wb7 = [S(p7, f"wb7_{i}", [128, C], BF16) for i in range(2)]
    qv7 = [S(p7, f"qv7_{i}", [128, T], BF16) for i in range(2)]
    vt7 = S(p7, "vt7", [128, 8 * T], BF16)
    psQ = [P(p7, f"psQ{i}", [128, 1024], F32) for i in range(3)]
    ptV = P(p7, "ptV", [128, 1024], BF16)

    p7_ld, p7_cast, p7_mm, p7_ev, p7_kst = [], [], [], [], []
    p7_vtr, p7_vev = [], []
    for j in range(F3C):
        for hf in range(2):
            @block.sync
            def _(sync, j=j, hf=hf):
                if j == 0 and hf == 0:
                    bar_waits(sync, p75_end, "VATG")
                n = 2 * j + hf
                if n >= 2:
                    sync.wait_ge(sA.sem, p7_cast[n - 2])
                p7_ld.append(sQ.dma(sync.dma_start(
                    out=wq7[n % 2][:],
                    in_=wqkv[j, :, hf * 2048:(hf + 1) * 2048])))

            @block.scalar
            def _(scalar, j=j, hf=hf):
                n = 2 * j + hf
                scalar.wait_ge(sQ.sem, p7_ld[n])
                if hf == 0 and j >= 2:
                    scalar.wait_ge(sT.sem, p7_mm[j - 2])
                p7_cast.append(sA.inc(scalar.activation(
                    wb7[j % 2][:, hf * 2048:(hf + 1) * 2048],
                    wq7[n % 2][:], AF.Identity)))

        @block.tensor
        def _(tensor, j=j):
            tensor.wait_ge(sA.sem, p7_cast[2 * j + 1])
            if j == 0:
                bar_waits(tensor, p75_end, "V")
            if j >= 3:
                tensor.wait_ge(sA.sem, p7_ev[j - 3])
            ps = psQ[j % 3]
            for th in range(2):
                for cc in range(CCn):
                    mm = tensor.matmul(
                        ps[:, th * 512:(th + 1) * 512],
                        wb7[j % 2][:, cc * 128:(cc + 1) * 128],
                        xnbf_fm[:, cc * T + th * 512:cc * T + th * 512 + 512],
                        start=(cc == 0), stop=(cc == CCn - 1))
            p7_mm.append(sT.inc(mm))

        @block.scalar
        def _(scalar, j=j):
            scalar.wait_ge(sT.sem, p7_mm[j])
            if j < 8:
                dst = q_sb[:, j * T:(j + 1) * T]
            else:
                dst = qv7[j % 2][:]
                if j >= 10:
                    if j - 2 < 16:
                        scalar.wait_ge(sG.sem, p7_kst[j - 2 - 8])
                    else:
                        scalar.wait_ge(sV.sem, p7_vev[(j - 2 - 16) * TC + TC - 1])
            p7_ev.append(sA.inc(scalar.activation(
                dst, psQ[j % 3][:], AF.Identity, bias=bqkv_c[:, j:j + 1])))

        if 8 <= j < 16:
            @block.gpsimd
            def _(gpsimd, j=j):
                gpsimd.wait_ge(sA.sem, p7_ev[j])
                p7_kst.append(sG.dma(gpsimd.dma_start(
                    out=kv_in[(j - 8) * 128:(j - 7) * 128, :],
                    in_=qv7[j % 2][:])))

        if j >= 16:
            for tc in range(TC):
                @block.tensor
                def _(tensor, j=j, tc=tc):
                    n = (j - 16) * TC + tc
                    if tc == 0:
                        tensor.wait_ge(sA.sem, p7_ev[j])
                    if n >= 1:
                        tensor.wait_ge(sV.sem, p7_vev[n - 1])
                    p7_vtr.append(sT.inc(tensor.transpose(
                        ptV[:, 0:128],
                        qv7[j % 2][:, tc * 128:(tc + 1) * 128], ident_bf[:])))

                @block.vector
                def _(vector, j=j, tc=tc):
                    n = (j - 16) * TC + tc
                    vector.wait_ge(sT.sem, p7_vtr[n])
                    p7_vev.append(sV.inc(vector.tensor_copy(
                        vt7[:, tc * T + (j - 16) * 128:tc * T + (j - 15) * 128],
                        ptV[:, 0:128])))

    @block.gpsimd
    def _(gpsimd):
        gpsimd.wait_ge(sV.sem, p7_vev[-1])
        gpsimd.wait_ge(sG.sem, p7_kst[-1])
        sG.dma(gpsimd.dma_start(
            out=bass.AP(kv_in, 1024 * D, [[D, 128], [128 * D, TC], [1, D]]),
            in_=vt7[:]))
        gpsimd.wait_ge(sG.sem, sG.v)
        sCC.cc(gpsimd.collective_compute(
            "AllGather", ALU.bypass, replica_groups=G4,
            ins=[kv_in[:, :]], outs=[kv_out[:, :]]))
        sKVF.cc(gpsimd.collective_compute(
            "AllGather", ALU.bypass, replica_groups=G4,
            ins=[kv_in[0:64, :]], outs=[kvf_out[:, :]]))

    p7_end = snap()
    p7.close()

    if stage < 4:
        return _finish(nc, block, dbg, debug, locals())

    p9 = ExitStack()
    w1f = [S(p9, f"w1f{i}", [128, 2048], F32) for i in range(2)]
    w1qb = [S(p9, f"w1qb{i}", [128, C], BF16) for i in range(2)]
    w1rb = [S(p9, f"w1rb{i}", [128, C], BF16) for i in range(2)]
    rw9 = S(p9, "rw9", [128, 256], BF16)
    c9 = S(p9, "c9", [128, 2], F32)
    row9 = S(p9, "row9", [1, 256], F32)
    hst = [S(p9, f"hst{i}", [128, T], F32) for i in range(2)]
    psF = [P(p9, f"psF{i}", [128, 1024], F32) for i in range(3)]
    ptg = P(p9, "ptg", [128, 1024], BF16)
    pbf = P(p9, "pbf", [128, 512], F32)

    p9_xo = []
    for cc in range(CCn):
        @block.vector
        def _(vector, cc=cc):
            if cc == 0:
                bar_waits(vector, p7_end, "T")
            p9_xo.append(sV.inc(vector.tensor_scalar(
                xnbf_fm[:, cc * T:(cc + 1) * T],
                xnbf_fm[:, cc * T:(cc + 1) * T],
                mask1_c[:, cc:cc + 1], None, ALU.mult)))
    v_xout = sV.v

    p9_ld, p9_mm, p9_ev, p9_hst, p9_cm = [], [], [], [], []
    p9_c, p9_tree, p9_t1, p9_rd, p9_t2, p9_rw, p9_bc, p9_q = \
        [], [], [], [], [], [], [], []
    for g in range(FCn):
        for hf in range(2):
            @block.sync
            def _(sync, g=g, hf=hf):
                if g == 0 and hf == 0:
                    bar_waits(sync, p7_end, "VATG")
                n = 2 * g + hf
                if n >= 2:
                    sync.wait_ge(sA.sem, p9_c[n - 2])
                p9_ld.append(sQ.dma(sync.dma_start(
                    out=w1f[n % 2][:],
                    in_=w1full[g * 128:(g + 1) * 128,
                               hf * 2048:(hf + 1) * 2048])))

            @block.scalar
            def _(scalar, g=g, hf=hf):
                n = 2 * g + hf
                scalar.wait_ge(sQ.sem, p9_ld[n])
                if hf == 0 and g >= 2:
                    scalar.wait_ge(sT.sem, p9_mm[g - 2])
                p9_c.append(sA.inc(scalar.activation(
                    w1rb[g % 2][:, hf * 2048:(hf + 1) * 2048],
                    w1f[n % 2][:], AF.Identity)))

        @block.vector
        def _(vector, g=g):
            vector.wait_ge(sA.sem, p9_c[2 * g + 1])
            b = w1rb[g % 2]
            q = w1qb[g % 2]
            if g >= 2:
                vector.wait_ge(sT.sem, p9_mm[g - 2])
            vector.tensor_tensor(q[:, 0:2048], b[:, 0:2048], b[:, 2048:4096],
                                 ALU.max)
            vector.scalar_tensor_tensor(q[:, 0:2048], b[:, 0:2048], -1.0,
                                        q[:, 0:2048], ALU.mult, ALU.max)
            vector.scalar_tensor_tensor(q[:, 0:2048], b[:, 2048:4096], -1.0,
                                        q[:, 0:2048], ALU.mult, ALU.max)
            vector.tensor_tensor(q[:, 2048:3072], q[:, 0:1024],
                                 q[:, 1024:2048], ALU.max)
            vector.tensor_tensor(q[:, 3072:3584], q[:, 2048:2560],
                                 q[:, 2560:3072], ALU.max)
            vector.tensor_tensor(q[:, 3584:3840], q[:, 3072:3328],
                                 q[:, 3328:3584], ALU.max)
            p9_tree.append(sV.inc(vector.tensor_tensor(
                q[:, 3840:3968], q[:, 3584:3712], q[:, 3712:3840], ALU.max)))

        @block.tensor
        def _(tensor, g=g):
            tensor.wait_ge(sV.sem, p9_tree[g])
            if g >= 1:
                tensor.wait_ge(sV.sem, p9_rd[g - 1])
            p9_t1.append(sT.inc(tensor.transpose(
                ptg[:, 0:128], w1qb[g % 2][:, 3840:3968], ident_bf[:])))

        @block.vector
        def _(vector, g=g):
            vector.wait_ge(sT.sem, p9_t1[g])
            p9_rd.append(sV.inc(vector.reduce_max(
                c9[:, 0:1], ptg[:, 0:128], AX.X)))

        @block.tensor
        def _(tensor, g=g):
            tensor.wait_ge(sV.sem, p9_rd[g])
            if g >= 1:
                tensor.wait_ge(sV.sem, p9_rw[g - 1])
            p9_t2.append(sT.inc(tensor.transpose(
                pbf[0:1, 0:128], c9[:, 0:1], ident_f[:])))

        @block.vector
        def _(vector, g=g):
            vector.wait_ge(sT.sem, p9_t2[g])
            vsync(vector, vector.tensor_copy(row9[0:1, 0:128], pbf[0:1, 0:128]))
            vsync(vector, vector.tensor_scalar(
                row9[0:1, 0:128], row9[0:1, 0:128], INV127, SMIN,
                ALU.mult, ALU.max))
            p9_rw.append(sV.inc(vector.reciprocal(row9[0:1, 128:256],
                                                  row9[0:1, 0:128])))

        @block.tensor
        def _(tensor, g=g):
            tensor.wait_ge(sV.sem, p9_rw[g])
            if g >= 1:
                tensor.wait_ge(sV.sem, p9_bc[g - 1])
            tensor.matmul(pbf[:, 0:128], ones_row[:, :], row9[0:1, 128:256],
                          start=True, stop=True)
            p9_bc.append(sT.inc(tensor.matmul(
                pbf[:, 128:256], ones_row[:, :], row9[0:1, 0:128],
                start=True, stop=True)))

        @block.vector
        def _(vector, g=g):
            vector.wait_ge(sT.sem, p9_bc[g])
            vector.tensor_copy(rw9[:, 0:128], pbf[:, 0:128])
            vector.tensor_copy(rw9[:, 128:256], pbf[:, 128:256])
            b3 = w1rb[g % 2][:].rearrange("p (a b) -> p a b", a=CCn, b=128)
            q3 = w1qb[g % 2][:].rearrange("p (a b) -> p a b", a=CCn, b=128)
            rb_ = bass.AP(rw9, 0, [[256, 128], [0, CCn], [1, 128]])
            sb_ = bass.AP(rw9, 128, [[256, 128], [0, CCn], [1, 128]])
            vector.tensor_tensor(q3, b3, rb_, ALU.mult)
            vector.tensor_scalar(w1qb[g % 2][:], w1qb[g % 2][:], MAGIC, MAGIC,
                                 ALU.add, ALU.subtract)
            p9_q.append(sV.inc(vector.tensor_tensor(q3, q3, sb_, ALU.mult)))

        @block.tensor
        def _(tensor, g=g):
            tensor.wait_ge(sV.sem, p9_q[g])
            if g == 0:
                tensor.wait_ge(sV.sem, v_xout)
                bar_waits(tensor, p7_end, "VA")
            if g >= 3:
                tensor.wait_ge(sA.sem, p9_ev[g - 3])
            ps = psF[g % 3]
            for th in range(2):
                for cc in range(CCn):
                    for v, (blk, act) in enumerate(
                            ((w1qb[g % 2], xq_fm), (w1rb[g % 2], xnbf_fm))):
                        mm = tensor.matmul(
                            ps[:, th * 512:(th + 1) * 512],
                            blk[:, cc * 128:(cc + 1) * 128],
                            act[:, cc * T + th * 512:cc * T + th * 512 + 512],
                            start=(cc == 0 and v == 0),
                            stop=(cc == CCn - 1 and v == 1))
            p9_mm.append(sT.inc(mm))

        @block.scalar
        def _(scalar, g=g):
            scalar.wait_ge(sT.sem, p9_mm[g])
            if g >= 2:
                scalar.wait_ge(sD.sem, p9_hst[g - 2])
                scalar.wait_ge(sV.sem, p9_cm[g - 2])
            p9_ev.append(sA.inc(scalar.activation(
                hst[g % 2][:], psF[g % 3][:], AF.Identity,
                bias=b1_c[:, g:g + 1])))
            scalar.wait_ge(sA.sem, p9_ev[g])
            p9_hst.append(sD.dma(scalar.dma_start(
                out=h_d[g * 128:(g + 1) * 128, :], in_=hst[g % 2][:])))

        @block.vector
        def _(vector, g=g):
            vector.wait_ge(sA.sem, p9_ev[g])
            p9_cm.append(sV.inc(vector.reduce_max(
                cmh[:, g:g + 1], hst[g % 2][:], AX.X,
                apply_absolute_value=True)))

    @block.gpsimd
    def _(gpsimd):
        gpsimd.wait_ge(sV.sem, p9_cm[-1])
        sG.dma(gpsimd.dma_start(out=ar3_in[:, :], in_=cmh[:]))
        gpsimd.wait_ge(sG.sem, sG.v)
        sCC.cc(gpsimd.collective_compute(
            "AllReduce", ALU.max, replica_groups=G8,
            ins=[ar3_in[:, :]], outs=[ar3_out[:, :]]))
    cc_ar3 = sCC.v

    p9_end = snap()
    p9.close()
    s_act.close()

    if stage < 5:
        return _finish(nc, block, dbg, debug, locals())

    p10 = ExitStack()
    avfm = S(p10, "avfm", [128, 8 * T], BF16)
    p10a = ExitStack()
    k_fm = S(p10a, "k_fm", [128, 8 * C], BF16)
    PT = S(p10a, "PT", [128, 32 * 512], BF16)
    sc_sb = [S(p10a, f"sc{i}", [128, C], F32) for i in range(2)]
    Pb = [S(p10a, f"Pb{i}", [128, C], BF16) for i in range(2)]
    vsb = [S(p10a, f"vsb{i}", [128, T], BF16) for i in range(2)]
    c10 = S(p10a, "c10", [128, 8], F32)
    psS = P(p10a, "psS", [128, 1024], F32)
    ptP = [P(p10a, f"ptP{i}", [128, 1024], BF16) for i in range(2)]
    psAV = [P(p10a, f"psAV{i}", [128, 512], F32) for i in range(4)]

    p10_kld = []

    @block.sync
    def _(sync):
        bar_waits(sync, p9_end, "VATD")
        sync.wait_ge(sKVF.sem, 1)
        for dc in range(8):
            for rb in range(4):
                sQ.dma(sync.dma_start(
                    out=k_fm[:, dc * C + rb * T:dc * C + rb * T + T],
                    in_=kv_out[rb * 2048 + dc * 128:rb * 2048 + (dc + 1) * 128, :]))
        p10_kld.append(sQ.v)
    q_kld = sQ.v

    p10_smm = {}
    p10_sev = {}
    p10_sm = {}
    p10_ptr = {}
    p10_pev = {}
    p10_avmm = {}
    p10_avev = {}

    def emit_scores(qc):
        for kq in range(4):
            @block.tensor
            def _(tensor, qc=qc, kq=kq):
                if qc == 0 and kq == 0:
                    tensor.wait_ge(sQ.sem, q_kld)
                    bar_waits(tensor, p9_end, "VA")
                prev = p10_sev.get((qc - (1 if kq == 0 else 0),
                                    (kq - 1) % 4))
                if prev:
                    tensor.wait_ge(sA.sem, prev)
                for d in range(8):
                    for n2 in range(2):
                        mm = tensor.matmul(
                            psS[:, n2 * 512:(n2 + 1) * 512],
                            q_sb[:, d * T + qc * 128:d * T + (qc + 1) * 128],
                            k_fm[:, d * C + kq * 1024 + n2 * 512:
                                 d * C + kq * 1024 + n2 * 512 + 512],
                            start=(d == 0), stop=(d == 7))
                p10_smm[(qc, kq)] = sT.inc(mm)

            @block.scalar
            def _(scalar, qc=qc, kq=kq):
                scalar.wait_ge(sT.sem, p10_smm[(qc, kq)])
                if qc >= 2 and kq == 0:
                    scalar.wait_ge(sV.sem, p10_sm[qc - 2])
                p10_sev[(qc, kq)] = sA.inc(scalar.activation(
                    sc_sb[qc % 2][:, kq * 1024:(kq + 1) * 1024],
                    psS[:], AF.Identity))

        @block.vector
        def _(vector, qc=qc):
            vector.wait_ge(sA.sem, p10_sev[(qc, 3)])
            vsync(vector, vector.reduce_max(c10[:, 0:1], sc_sb[qc % 2][:], AX.X))
            sV.inc(vector.tensor_scalar(c10[:, 1:2], c10[:, 0:1], -ISQ, None,
                                        ALU.mult))
        vmax = sV.v

        @block.scalar
        def _(scalar, qc=qc):
            scalar.wait_ge(sV.sem, vmax)
            if qc >= 2:
                scalar.wait_ge(sT.sem, p10_ptr[(qc - 2, CCn - 1)])
            sA.inc(scalar.activation(Pb[qc % 2][:], sc_sb[qc % 2][:], AF.Exp,
                                     bias=c10[:, 1:2], scale=ISQ,
                                     accum_out=c10[:, 2:3]))
        aexp = sA.v

        @block.vector
        def _(vector, qc=qc):
            vector.wait_ge(sA.sem, aexp)
            vsync(vector, vector.reciprocal(c10[:, 3:4], c10[:, 2:3]))
            p10_sm[qc] = sV.inc(vector.tensor_scalar(
                Pb[qc % 2][:], Pb[qc % 2][:], c10[:, 3:4], None, ALU.mult))

        for kc in range(CCn):
            @block.tensor
            def _(tensor, qc=qc, kc=kc):
                if kc == 0:
                    tensor.wait_ge(sV.sem, p10_sm[qc])
                    if qc == 4:
                        tensor.wait_ge(sT.sem, p10_avmm[1 * CCn + CCn - 1])
                prev = p10_pev.get((qc, kc - 2)) or p10_pev.get((qc - 1, CCn - 2 + kc))
                if prev:
                    tensor.wait_ge(sV.sem, prev)
                p10_ptr[(qc, kc)] = sT.inc(tensor.transpose(
                    ptP[kc % 2][:, 0:128],
                    Pb[qc % 2][:, kc * 128:(kc + 1) * 128], ident_bf[:]))

            @block.vector
            def _(vector, qc=qc, kc=kc):
                vector.wait_ge(sT.sem, p10_ptr[(qc, kc)])
                p10_pev[(qc, kc)] = sV.inc(vector.tensor_copy(
                    PT[:, kc * 512 + (qc % 4) * 128:kc * 512 + (qc % 4) * 128 + 128],
                    ptP[kc % 2][:, 0:128]))

    p10_vld = {}

    def emit_av(half):
        for dh in range(2):
            for kc in range(CCn):
                @block.sync
                def _(sync, half=half, dh=dh, kc=kc):
                    if dh == 0 and kc == 0:
                        sync.wait_ge(sV.sem, p10_pev[(half * 4 + 3, CCn - 1)])
                    n = (half * 2 + dh) * CCn + kc
                    if n >= 2:
                        sync.wait_ge(sT.sem, p10_avmm[n - 2])
                    rb, tl = kc // 8, (kc % 8) * 128
                    p10_vld[n] = sQ.dma(sync.dma_start(
                        out=vsb[kc % 2][:],
                        in_=kv_out[rb * 2048 + 1024 + tl:
                                   rb * 2048 + 1024 + tl + 128, :]))

                @block.tensor
                def _(tensor, half=half, dh=dh, kc=kc):
                    n = (half * 2 + dh) * CCn + kc
                    tensor.wait_ge(sQ.sem, p10_vld[n])
                    if kc == 0 and (half * 2 + dh) >= 1:
                        tensor.wait_ge(sA.sem, p10_avev[half * 2 + dh - 1])
                    for dvl in range(4):
                        dv = dh * 4 + dvl
                        mm = tensor.matmul(
                            psAV[dvl][:, 0:512],
                            vsb[kc % 2][:, dv * 128:(dv + 1) * 128],
                            PT[:, kc * 512:(kc + 1) * 512],
                            start=(kc == 0), stop=(kc == CCn - 1))
                    p10_avmm[n] = sT.inc(mm)

            @block.scalar
            def _(scalar, half=half, dh=dh):
                n = (half * 2 + dh) * CCn + CCn - 1
                scalar.wait_ge(sT.sem, p10_avmm[n])
                for dvl in range(4):
                    dv = dh * 4 + dvl
                    sA.inc(scalar.activation(
                        avfm[:, dv * T + half * 512:dv * T + half * 512 + 512],
                        psAV[dvl][:], AF.Identity))
                p10_avev[half * 2 + dh] = sA.v

    for qc in range(4):
        emit_scores(qc)
    emit_av(0)
    for qc in range(4, 8):
        emit_scores(qc)
    emit_av(1)

    p10a_end = snap()
    p10a.close()

    p10b = ExitStack()
    wof = [S(p10b, f"wof{i}", [128, 1024], F32) for i in range(2)]
    wo_bf = S(p10b, "wo_bf", [128, 8 * C], BF16)
    at_sb = [S(p10b, f"at{i}", [128, C], F32) for i in range(2)]
    xst = [S(p10b, f"xst{i}", [128, C], F32) for i in range(2)]
    bob2b = S(p10b, "bob2b", [128, C], BF16)
    rowB10 = S(p10b, "rowB10", [1, 2048], F32)

    pbias = ExitStack()
    pbc10 = P(pbias, "pbc10", [128, 2048], F32)
    p10_bias_mm, p10_bias_ld = [], []
    for half in range(2):
        @block.sync
        def _(sync, half=half):
            if half == 0:
                bar_waits(sync, p10a_end, "VATD")
            else:
                sync.wait_ge(sT.sem, p10_bias_mm[-1])
            p10_bias_ld.append(sR.dma(sync.dma_start(
                out=rowB10[0:1, :],
                in_=bo_row[0:1, half * 2048:(half + 1) * 2048])))

        @block.tensor
        def _(tensor, half=half):
            tensor.wait_ge(sR.sem, p10_bias_ld[-1])
            if half == 0:
                bar_waits(tensor, p10a_end, "VA")
            for i in range(4):
                mm = tensor.matmul(pbc10[:, i * 512:(i + 1) * 512],
                                   ones_row[:, :],
                                   rowB10[0:1, i * 512:(i + 1) * 512],
                                   start=True, stop=False)
            p10_bias_mm.append(sT.inc(mm))

        @block.sync
        def _(sync, half=half):
            sync.wait_ge(sT.sem, p10_bias_mm[-1])
            p10_bias_ld.append(sR.dma(sync.dma_start(
                out=rowB10[0:1, :],
                in_=b2_row[0:1, half * 2048:(half + 1) * 2048])))

        @block.tensor
        def _(tensor, half=half):
            tensor.wait_ge(sR.sem, p10_bias_ld[-1])
            for i in range(4):
                mm = tensor.matmul(pbc10[:, i * 512:(i + 1) * 512],
                                   ones_row[:, :],
                                   rowB10[0:1, i * 512:(i + 1) * 512],
                                   start=False, stop=True)
            p10_bias_mm.append(sT.inc(mm))

        @block.vector
        def _(vector, half=half):
            vector.wait_ge(sT.sem, p10_bias_mm[-1])
            sV.inc(vector.tensor_copy(bob2b[:, half * 2048:(half + 1) * 2048],
                                      pbc10[:]))
    v_bias10 = sV.v
    pbias.close()

    p10c = ExitStack()
    psP2 = [P(p10c, f"psP{i}", [128, 2048], F32) for i in range(2)]

    p10_wld, p10_wc = [], []
    for dv in range(8):
        for qt in range(4):
            @block.sync
            def _(sync, dv=dv, qt=qt):
                n = dv * 4 + qt
                if n >= 2:
                    sync.wait_ge(sA.sem, p10_wc[n - 2])
                p10_wld.append(sQ.dma(sync.dma_start(
                    out=wof[n % 2][:],
                    in_=wo[dv * 128:(dv + 1) * 128,
                           qt * 1024:(qt + 1) * 1024])))

            @block.scalar
            def _(scalar, dv=dv, qt=qt):
                n = dv * 4 + qt
                scalar.wait_ge(sQ.sem, p10_wld[n])
                p10_wc.append(sA.inc(scalar.activation(
                    wo_bf[:, dv * C + qt * 1024:dv * C + (qt + 1) * 1024],
                    wof[n % 2][:], AF.Identity)))

    p10_pmm, p10_pec, p10_ast, p10_xld, p10_res = {}, {}, [], [], {}
    for qc in range(8):
        @block.sync
        def _(sync, qc=qc):
            if qc >= 2:
                sync.wait_ge(sV.sem, p10_res[qc - 2])
            p10_xld.append(sQ.dma(sync.dma_start(
                out=xst[qc % 2][:], in_=x_my[qc * 128:(qc + 1) * 128, :])))

        for hh in range(2):
            @block.tensor
            def _(tensor, qc=qc, hh=hh):
                n = qc * 2 + hh
                if n == 0:
                    tensor.wait_ge(sA.sem, p10_wc[-1])
                    tensor.wait_ge(sA.sem, p10_avev[3])
                    bar_waits(tensor, p10a_end, "V")
                if n >= 2:
                    tensor.wait_ge(sA.sem, p10_pec[n - 2])
                for dv in range(8):
                    for hg in range(4):
                        mm = tensor.matmul(
                            psP2[n % 2][:, hg * 512:(hg + 1) * 512],
                            avfm[:, dv * T + qc * 128:dv * T + (qc + 1) * 128],
                            wo_bf[:, dv * C + hh * 2048 + hg * 512:
                                  dv * C + hh * 2048 + hg * 512 + 512],
                            start=(dv == 0), stop=(dv == 7))
                p10_pmm[n] = sT.inc(mm)

            @block.scalar
            def _(scalar, qc=qc, hh=hh):
                n = qc * 2 + hh
                scalar.wait_ge(sT.sem, p10_pmm[n])
                if qc >= 2 and hh == 0:
                    scalar.wait_ge(sV.sem, p10_res[qc - 2])
                p10_pec[n] = sA.inc(scalar.activation(
                    at_sb[qc % 2][:, hh * 2048:(hh + 1) * 2048],
                    psP2[n % 2][:], AF.Identity))

        @block.vector
        def _(vector, qc=qc):
            vector.wait_ge(sA.sem, p10_pec[qc * 2 + 1])
            vector.wait_ge(sQ.sem, p10_xld[qc])
            if qc == 0:
                vector.wait_ge(sV.sem, v_bias10)
            if qc >= 2:
                vector.wait_ge(sD.sem, p10_ast[qc - 2])
            a = at_sb[qc % 2][:]
            vector.tensor_add(a, a, xst[qc % 2][:])
            p10_res[qc] = sV.inc(vector.tensor_tensor(a, a, bob2b[:], ALU.add))

        @block.scalar
        def _(scalar, qc=qc):
            scalar.wait_ge(sV.sem, p10_res[qc])
            p10_ast.append(sD.dma(scalar.dma_start(
                out=attn_sp[qc * 128:(qc + 1) * 128, :], in_=at_sb[qc % 2][:])))

    p10c.close()
    p10_end = snap()
    p10b.close()
    p10.close()
    s_q.close()

    if stage < 6:
        return _finish(nc, block, dbg, debug, locals())

    @block.gpsimd
    def _(gpsimd):
        gpsimd.wait_ge(sCC.sem, cc_ar3)
        sG.dma(gpsimd.dma_start(out=cmh[:], in_=ar3_out[:, :]))
    g_cmh = sG.v

    @block.vector
    def _(vector):
        vector.wait_ge(sG.sem, g_cmh)
        vsync(vector, vector.tensor_scalar(mask2_c[:], cmh[:], THR, None,
                                           ALU.is_gt))
        sV.inc(vector.tensor_scalar(mi2_c[:], mask2_c[:], -1.0, 1.0,
                                    ALU.mult, ALU.add))
    v_mask2 = sV.v

    p11 = ExitStack()
    hA = [S(p11, f"hA{i}", [128, T], F32) for i in range(2)]
    hAb = [S(p11, f"hAb{i}", [128, T], F32) for i in range(2)]
    rm2acc = S(p11, "rm2acc", [128, T], F32)
    pr2 = [P(p11, f"pr2_{i}", [128, 512], F32) for i in range(2)]

    @block.vector
    def _(vector):
        bar_waits(vector, p10_end, "AD")
        sV.inc(vector.memset(rm2acc[:], 0.0))

    p11_ld, p11_ab, p11_ac = [], [], []
    for g in range(FCn):
        @block.sync
        def _(sync, g=g):
            if g == 0:
                bar_waits(sync, p10_end, "VATD")
                sync.wait_ge(sD.sem, p9_hst[-1])
            if g >= 2:
                sync.wait_ge(sV.sem, p11_ac[g - 2])
            p11_ld.append(sQ.dma(sync.dma_start(
                out=hA[g % 2][:], in_=h_d[g * 128:(g + 1) * 128, :])))

        @block.scalar
        def _(scalar, g=g):
            scalar.wait_ge(sQ.sem, p11_ld[g])
            if g == 0:
                bar_waits(scalar, p10_end, "VD")
            if g >= 2:
                scalar.wait_ge(sV.sem, p11_ac[g - 2])
            p11_ab.append(sA.inc(scalar.activation(
                hAb[g % 2][:], hA[g % 2][:], AF.Abs)))

        @block.vector
        def _(vector, g=g):
            vector.wait_ge(sA.sem, p11_ab[g])
            if g == 0:
                vector.wait_ge(sV.sem, v_mask2)
            p11_ac.append(sV.inc(vector.scalar_tensor_tensor(
                rm2acc[:], hAb[g % 2][:], mi2_c[:, g:g + 1], rm2acc[:],
                ALU.mult, ALU.max)))

    p11_tr, p11_rd = [], []
    for tcb in range(TC):
        @block.tensor
        def _(tensor, tcb=tcb):
            if tcb == 0:
                tensor.wait_ge(sV.sem, p11_ac[-1])
                bar_waits(tensor, p10_end, "VA")
            if tcb >= 2:
                tensor.wait_ge(sV.sem, p11_rd[tcb - 2])
            p11_tr.append(sT.inc(tensor.transpose(
                pr2[tcb % 2][:, 0:128], rm2acc[:, tcb * 128:(tcb + 1) * 128],
                ident_f[:])))

        @block.vector
        def _(vector, tcb=tcb):
            vector.wait_ge(sT.sem, p11_tr[tcb])
            p11_rd.append(sV.inc(vector.reduce_max(
                rm2c[:, tcb:tcb + 1], pr2[tcb % 2][:, 0:128], AX.X)))

    @block.vector
    def _(vector):
        vector.wait_ge(sV.sem, p11_rd[-1])
        sV.inc(vector.tensor_scalar(rm2c[:], rm2c[:], INV127, SMIN,
                                    ALU.mult, ALU.max))
    v_sx2c = sV.v

    @block.tensor
    def _(tensor):
        tensor.wait_ge(sV.sem, v_sx2c)
        sT.inc(tensor.transpose(pr2[0][0:TC, 0:128], rm2c[:], ident_f[:]))
    t_sx2t = sT.v

    @block.vector
    def _(vector):
        vector.wait_ge(sT.sem, t_sx2t)
        sV.inc(vector.tensor_copy(swr16[0:TC, 0:128], pr2[0][0:TC, 0:128]))
    v_sx2t = sV.v

    @block.gpsimd
    def _(gpsimd):
        gpsimd.wait_ge(sV.sem, v_sx2t)
        sGR.dma(gpsimd.dma_start(
            out=bnc[0:1, 0:T].rearrange("a (r c) -> (a r) c", r=TC, c=128),
            in_=swr16[0:TC, 0:128]))
    g_sx2b = sGR.v

    p11_end = snap()
    p11.close()

    pB = ExitStack()
    rowE = S(pB, "rowE", [1, T], F32)
    rowF = S(pB, "rowF", [1, T], F32)
    sx2b = S(pB, "sx2b", [128, T], F32)
    rsx2b = S(pB, "rsx2b", [128, T], F32)
    hfp = [S(pB, f"hfp{i}", [128, T], F32) for i in range(2)]
    htm = [S(pB, f"htm{i}", [128, T], F32) for i in range(2)]
    hqt = [S(pB, f"hqt{i}", [128, T], BF16) for i in range(2)]
    hot = [S(pB, f"hot{i}", [128, T], BF16) for i in range(2)]
    pbc5 = P(pB, "pbc5", [128, 512], F32)

    @block.sync
    def _(sync):
        bar_waits(sync, p11_end, "VAT")
        sync.wait_ge(sGR.sem, g_sx2b)
        sR.dma(sync.dma_start(out=rowE[0:1, 0:T], in_=bnc[0:1, 0:T]))
    q_sx2r = sR.v

    @block.vector
    def _(vector):
        vector.wait_ge(sR.sem, q_sx2r)
        sV.inc(vector.reciprocal(rowF[0:1, 0:T], rowE[0:1, 0:T]))
    v_sx2r = sV.v

    bc5_mm, bc5_ev = [], []
    bcastp(rowE, sx2b, T, pbc5, bc5_mm, bc5_ev,
           first_waits=((sV.sem, v_sx2r), (sT.sem, p11_end["T"])))
    bcastp(rowF, rsx2b, T, pbc5, bc5_mm, bc5_ev)
    v_sxb = sV.v

    pB_ld, pB_q, pB_o, pB_st = [], [], [], []
    for fc in range(FCn):
        @block.sync
        def _(sync, fc=fc):
            if fc == 0:
                sync.wait_ge(sV.sem, v_sxb)
            if fc >= 2:
                sync.wait_ge(sV.sem, pB_q[fc - 2])
                sync.wait_ge(sA.sem, pB_o[fc - 2])
            sQ.dma(sync.dma_start(out=hfp[fc % 2][:],
                                  in_=h_d[fc * 128:(fc + 1) * 128, :]))
            pB_ld.append(sQ.v)

        @block.vector
        def _(vector, fc=fc):
            vector.wait_ge(sQ.sem, pB_ld[fc])
            if fc >= 2:
                vector.wait_ge(sD.sem, pB_st[fc - 2])
            vector.scalar_tensor_tensor(htm[fc % 2][:], hfp[fc % 2][:],
                                        mi2_c[:, fc:fc + 1], rsx2b[:],
                                        ALU.mult, ALU.mult)
            vector.tensor_scalar(htm[fc % 2][:], htm[fc % 2][:], MAGIC, MAGIC,
                                 ALU.add, ALU.subtract)
            pB_q.append(sV.inc(vector.tensor_tensor(
                hqt[fc % 2][:], htm[fc % 2][:], sx2b[:], ALU.mult)))

        @block.scalar
        def _(scalar, fc=fc):
            scalar.wait_ge(sQ.sem, pB_ld[fc])
            if fc >= 2:
                scalar.wait_ge(sD.sem, pB_st[fc - 2])
            pB_o.append(sA.inc(scalar.activation(
                hot[fc % 2][:], hfp[fc % 2][:], AF.Identity,
                scale=mask2_c[:, fc:fc + 1])))
            scalar.wait_ge(sA.sem, pB_o[fc])
            scalar.wait_ge(sV.sem, pB_q[fc])
            sD.dma(scalar.dma_start(out=hq_d[fc * 128:(fc + 1) * 128, :],
                                    in_=hqt[fc % 2][:]))
            pB_st.append(sD.dma(scalar.dma_start(
                out=ho_d[fc * 128:(fc + 1) * 128, :], in_=hot[fc % 2][:])))

    pB_end = snap()
    pB.close()

    p12 = ExitStack()
    acc = S(p12, "acc", [128, 8 * 1024], F32)
    w2qB = [S(p12, f"w2qB{i}", [128, 8 * 1024], BF16) for i in range(2)]
    w2rB = [S(p12, f"w2rB{i}", [128, 8 * 1024], BF16) for i in range(2)]
    hqB = [S(p12, f"hqB{i}", [128, 8 * 1024], BF16) for i in range(2)]
    hoB = [S(p12, f"hoB{i}", [128, 8 * 1024], BF16) for i in range(2)]
    w2fp = [S(p12, f"w2fp{i}", [128, 1024], F32) for i in range(2)]
    rsw2bf = S(p12, "rsw2bf", [128, C], BF16)
    sw2bf = S(p12, "sw2bf", [128, C], BF16)

    pb6 = ExitStack()
    rowH = S(pb6, "rowH", [1, C], F32)
    pbc6 = P(pb6, "pbc6", [128, C], F32)

    @block.gpsimd
    def _(gpsimd):
        gpsimd.wait_ge(sCC.sem, cc_ar0)
        bar_waits(gpsimd, pB_end, "VATD")
        sG.dma(gpsimd.dma_start(out=sw2loc[:], in_=ar0_out[:, :]))
    g_sw2g = sG.v

    @block.tensor
    def _(tensor):
        tensor.wait_ge(sG.sem, g_sw2g)
        bar_waits(tensor, pB_end, "VA")
        sT.inc(tensor.transpose(pbc6[0:CCn, 0:128], sw2loc[:], ident_f[:]))
    t_sw2t = sT.v

    @block.vector
    def _(vector):
        vector.wait_ge(sT.sem, t_sw2t)
        bar_waits(vector, pB_end, "AD")
        sV.inc(vector.tensor_copy(swr16[0:CCn, 0:128], pbc6[0:CCn, 0:128]))
    v_sw2t = sV.v

    @block.gpsimd
    def _(gpsimd):
        gpsimd.wait_ge(sV.sem, v_sw2t)
        sGR.dma(gpsimd.dma_start(
            out=bnc[0:1, 0:C].rearrange("a (r c) -> (a r) c", r=CCn, c=128),
            in_=swr16[0:CCn, 0:128]))
    g_sw2b = sGR.v

    @block.sync
    def _(sync):
        sync.wait_ge(sGR.sem, g_sw2b)
        sR.dma(sync.dma_start(out=rowH[0:1, 0:C], in_=bnc[0:1, 0:C]))
    q_sw2r = sR.v

    @block.vector
    def _(vector):
        vector.wait_ge(sR.sem, q_sw2r)
        sV.inc(vector.tensor_scalar(rowH[0:1, 0:C], rowH[0:1, 0:C],
                                    INV127, SMIN, ALU.mult, ALU.max))
    v_sw2r = sV.v

    bc6_mm, bc6_ev = [], []
    bcastp(rowH, sw2bf, C, pbc6, bc6_mm, bc6_ev,
           first_waits=((sV.sem, v_sw2r), (sT.sem, t_sw2t)))

    @block.vector
    def _(vector):
        vector.wait_ge(sV.sem, bc6_ev[-1])
        sV.inc(vector.reciprocal(rowH[0:1, 0:C], rowH[0:1, 0:C]))
    v_sw2ri = sV.v

    bcastp(rowH, rsw2bf, C, pbc6, bc6_mm, bc6_ev,
           first_waits=((sV.sem, v_sw2ri),))
    v_swb = sV.v

    pb6.close()

    psF2 = [[P(p12, f"psF2_{s}_{t}", [128, 512], F32) for t in range(4)]
            for s in range(2)]

    NFB = 16
    FPB = 8
    p12_ai = {}
    p12_wld = {}
    p12_wc = []
    p12_wq = []
    p12_mm = {}
    p12_fl = {}
    p12_ost = {}

    for cq in range(4):
        for tc in range(TC):
            @block.sync
            def _(sync, cq=cq, tc=tc):
                if cq == 0 and tc == 0:
                    sync.wait_ge(sD.sem, p10_ast[-1])
                if cq > 0:
                    sync.wait_ge(sG.sem, p12_ost[(cq - 1, tc)])
                p12_ai[(cq, tc)] = sQ.dma(sync.dma_start(
                    out=acc[:, tc * 1024:(tc + 1) * 1024],
                    in_=attn_sp[tc * 128:(tc + 1) * 128,
                                cq * 1024:(cq + 1) * 1024]))

        for fb in range(NFB):
            nfb = cq * NFB + fb

            for fc_in in range(FPB):
                fc = fb * FPB + fc_in

                @block.sync
                def _(sync, cq=cq, fb=fb, nfb=nfb, fc=fc, fc_in=fc_in):
                    if nfb >= 2 and fc_in == 0:
                        sync.wait_ge(sT.sem, p12_mm[(nfb - 2, 3)])
                    if cq == 0 and fc_in == 0:
                        sync.wait_ge(sD.sem, pB_st[fb * FPB + FPB - 1])
                    if nfb == 0 and fc_in == 0:
                        sync.wait_ge(sV.sem, v_swb)
                    n = nfb * FPB + fc_in
                    if n >= 2:
                        sync.wait_ge(sA.sem, p12_wc[n - 2])
                    sQ.dma(sync.dma_start(
                        out=w2fp[n % 2][:],
                        in_=w2full[fc * 128:(fc + 1) * 128,
                                   cq * 1024:(cq + 1) * 1024]))
                    sQ.dma(sync.dma_start(
                        out=hqB[fb % 2][:, fc_in * 1024:(fc_in + 1) * 1024],
                        in_=hq_d[fc * 128:(fc + 1) * 128, :]))
                    sQ.dma(sync.dma_start(
                        out=hoB[fb % 2][:, fc_in * 1024:(fc_in + 1) * 1024],
                        in_=ho_d[fc * 128:(fc + 1) * 128, :]))
                    p12_wld[("fc", nfb, fc_in)] = sQ.v

                @block.scalar
                def _(scalar, cq=cq, fb=fb, nfb=nfb, fc=fc, fc_in=fc_in):
                    n = nfb * FPB + fc_in
                    scalar.wait_ge(sQ.sem, p12_wld[("fc", nfb, fc_in)])
                    if nfb >= 2 and fc_in == 0:
                        scalar.wait_ge(sT.sem, p12_mm[(nfb - 2, 3)])
                    p12_wc.append(sA.inc(scalar.activation(
                        w2rB[fb % 2][:, fc_in * 1024:(fc_in + 1) * 1024],
                        w2fp[n % 2][:], AF.Identity)))

                @block.vector
                def _(vector, cq=cq, fb=fb, nfb=nfb, fc=fc, fc_in=fc_in):
                    n = nfb * FPB + fc_in
                    vector.wait_ge(sA.sem, p12_wc[n])
                    if nfb >= 2 and fc_in == 0:
                        vector.wait_ge(sT.sem, p12_mm[(nfb - 2, 3)])
                    qs = w2qB[fb % 2][:, fc_in * 1024:(fc_in + 1) * 1024]
                    rs = w2rB[fb % 2][:, fc_in * 1024:(fc_in + 1) * 1024]
                    cqs = slice(cq * 1024, (cq + 1) * 1024)
                    vector.tensor_tensor(qs, rs, rsw2bf[:, cqs], ALU.mult)
                    vector.tensor_scalar(qs, qs, MAGIC, MAGIC, ALU.add,
                                         ALU.subtract)
                    p12_wq.append(sV.inc(vector.tensor_tensor(
                        qs, qs, sw2bf[:, cqs], ALU.mult)))

            for grp in range(4):
                tq, ch = grp // 2, grp % 2
                bs = (nfb * 4 + grp) % 2

                @block.tensor
                def _(tensor, cq=cq, fb=fb, nfb=nfb, grp=grp, tq=tq, ch=ch,
                      bs=bs):
                    tensor.wait_ge(sQ.sem, p12_wld[("fc", nfb, FPB - 1)])
                    tensor.wait_ge(sV.sem, p12_wq[nfb * FPB + FPB - 1])
                    ngrp = nfb * 4 + grp
                    if ngrp >= 2:
                        tensor.wait_ge(sV.sem, p12_fl[ngrp - 2])
                    if nfb == 0 and grp == 0:
                        tensor.wait_ge(sV.sem, v_swb)
                    for fc_in in range(FPB):
                        for t in range(4):
                            for v, (hb, wb) in enumerate(
                                    ((hqB[fb % 2], w2qB[fb % 2]),
                                     (hoB[fb % 2], w2rB[fb % 2]))):
                                mm = tensor.matmul(
                                    psF2[bs][t][:, 0:512],
                                    hb[:, fc_in * 1024 + tq * 512 + t * 128:
                                       fc_in * 1024 + tq * 512 + (t + 1) * 128],
                                    wb[:, fc_in * 1024 + ch * 512:
                                       fc_in * 1024 + ch * 512 + 512],
                                    start=(fc_in == 0 and v == 0),
                                    stop=(fc_in == FPB - 1 and v == 1))
                    p12_mm[(nfb, grp)] = sT.inc(mm)

                @block.vector
                def _(vector, cq=cq, fb=fb, nfb=nfb, grp=grp, tq=tq, ch=ch,
                      bs=bs):
                    ngrp = nfb * 4 + grp
                    vector.wait_ge(sT.sem, p12_mm[(nfb, grp)])
                    if fb == 0:
                        for t in range(4):
                            vector.wait_ge(sQ.sem, p12_ai[(cq, tq * 4 + t)])
                    for t in range(4):
                        a = acc[:, (tq * 4 + t) * 1024 + ch * 512:
                                (tq * 4 + t) * 1024 + ch * 512 + 512]
                        fa = vector.tensor_add(a, a, psF2[bs][t][:])
                    p12_fl[ngrp] = sV.inc(fa)

        for tc in range(TC):
            @block.gpsimd
            def _(gpsimd, cq=cq, tc=tc):
                nfb = cq * NFB + NFB - 1
                gpsimd.wait_ge(sV.sem, p12_fl[nfb * 4 + 3])
                p12_ost[(cq, tc)] = sG.dma(gpsimd.dma_start(
                    out=out_my[tc * 128:(tc + 1) * 128,
                               cq * 1024:(cq + 1) * 1024],
                    in_=acc[:, tc * 1024:(tc + 1) * 1024]))

    return _finish(nc, block, dbg, debug, locals())


def _finish(nc, block, dbg, debug, env):
    sG, sD, sQ, sCC = env["sG"], env["sD"], env["sQ"], env["sCC"]
    sV, sA, sT = env["sV"], env["sA"], env["sT"]
    sR, sGR = env["sR"], env["sGR"]
    extra = [env.get("sFen"), env.get("sKVF")]

    if debug:
        pairs = []
        sbp = []
        t = env.get("xq_fm")
        if t is not None and "d_xq" in dbg and env.get("stage_sb_dump", True):
            try:
                sbp.append((dbg["d_xq"], t))
            except Exception:
                pass
        for name, src in [
            ("d_xn", env.get("xn_sp")), ("d_h", env.get("h_d")),
            ("d_attn", env.get("attn_sp")), ("d_hq", env.get("hq_d")),
            ("d_kv", env.get("kv_out")), ("d_w1ag", None),
            ("d_w2ag", None), ("d_cm1", env.get("ar1_out")),
            ("d_cmh", env.get("ar3_out")), ("d_snap", env.get("xnbf_snap")),
            ("d_attn", env.get("xnbf_snap2")) if env.get("stage", 9) < 5 else ("d_snap", env.get("xnbf_snap")),
        ]:
            if name in dbg and src is not None:
                pairs.append((dbg[name], src))

        @block.gpsimd
        def _(gpsimd):
            if sV.v:
                gpsimd.wait_ge(sV.sem, sV.v)
            if sA.v:
                gpsimd.wait_ge(sA.sem, sA.v)
            if sT.v:
                gpsimd.wait_ge(sT.sem, sT.v)
            if sD.v:
                gpsimd.wait_ge(sD.sem, sD.v)
            if sCC.v:
                gpsimd.wait_ge(sCC.sem, sCC.v)
            for d, s in pairs:
                sG.dma(gpsimd.dma_start(out=d[:, :], in_=s[:, :]))
            for d, s in sbp:
                sG.dma(gpsimd.dma_start(out=d[:, :], in_=s[:]))

    @block.gpsimd
    def _(gpsimd):
        for s_ in extra:
            if s_ is not None and s_.v:
                gpsimd.wait_ge(s_.sem, s_.v)
        if sR.v:
            gpsimd.wait_ge(sR.sem, sR.v)
        if sGR.v:
            gpsimd.wait_ge(sGR.sem, sGR.v)
        if sQ.v:
            gpsimd.wait_ge(sQ.sem, sQ.v)
        if sD.v:
            gpsimd.wait_ge(sD.sem, sD.v)
        if sCC.v:
            gpsimd.wait_ge(sCC.sem, sCC.v)
        if sV.v:
            gpsimd.wait_ge(sV.sem, sV.v)
        if sA.v:
            gpsimd.wait_ge(sA.sem, sA.v)
        if sT.v:
            gpsimd.wait_ge(sT.sem, sT.v)
        gpsimd.wait_ge(sG.sem, sG.v)
    return nc


def _host_layout(inputs):
    x = np.ascontiguousarray(
        np.asarray(inputs["L0_attn_x"], dtype=np.float32).reshape(8 * T, C))
    Wqkv = np.asarray(inputs["L0_attn_Wqkv"], dtype=np.float32)
    bqkv = np.asarray(inputs["L0_attn_bqkv"], dtype=np.float32)
    Wo = np.ascontiguousarray(np.asarray(inputs["L0_attn_Wo"], dtype=np.float32))
    bo = np.asarray(inputs["L0_attn_bo"], dtype=np.float32)
    W1 = np.asarray(inputs["L0_ffn_W1"], dtype=np.float32)
    b1 = np.asarray(inputs["L0_ffn_b1"], dtype=np.float32)
    W2 = np.asarray(inputs["L0_ffn_W2"], dtype=np.float32)
    b2 = np.asarray(inputs["L0_ffn_b2"], dtype=np.float32)

    wqkv_r = np.ascontiguousarray(
        Wqkv.reshape(CCn, 128, F3C, 128).transpose(2, 1, 0, 3)
    ).reshape(F3C, 128, C)
    bqkv_col = np.ascontiguousarray(bqkv.reshape(F3C, 128).T)
    b1_col = np.ascontiguousarray(b1.reshape(FCn, 128).T)
    bo_row = bo.reshape(1, C)
    b2_row = b2.reshape(1, C)
    identbf = np.eye(128, dtype=ml_dtypes.bfloat16)
    identf = np.eye(128, dtype=np.float32)

    w1full = np.ascontiguousarray(
        W1.reshape(FCn, 128, CCn, 128).transpose(0, 3, 2, 1)).reshape(FCn * 128, C)
    w2full = np.ascontiguousarray(W2.T)
    maps = []
    for c in range(8):
        maps.append({
            "x_my": np.ascontiguousarray(x[c * T:(c + 1) * T]),
            "wqkv": wqkv_r,
            "wo": Wo,
            "w1full": w1full,
            "w2full": w2full,
            "w2scan": w2full[c * FCL * 128:(c + 1) * FCL * 128],
            "bqkv_col": bqkv_col,
            "b1_col": b1_col,
            "bo_row": bo_row,
            "b2_row": b2_row,
            "identbf_d": identbf,
            "identf_d": identf,
        })
    return maps


def kernel(**inputs):
    maps = _host_layout(inputs)
    nc = build_program()
    res = run_bass_kernel_spmd(nc, maps, core_ids=list(range(8)))
    out = np.concatenate([np.asarray(res.results[i]["out_my"])
                          for i in range(8)], axis=0)
    return out.reshape(2, 4096, C)


# version 02
# speedup vs baseline: 1.0340x; 1.0340x over previous
import numpy as np
import ml_dtypes
from contextlib import ExitStack
import concourse.bass as bass
import concourse.mybir as mybir
from concourse.bass_utils import run_bass_kernel_spmd
from concourse.replica_groups import maybe_share_collective_output_space

F32 = mybir.dt.float32
BF16 = mybir.dt.bfloat16
AF = mybir.ActivationFunctionType
ALU = mybir.AluOpType
AX = mybir.AxisListType

MAGIC = float(np.float32(1.5 * 2**23))
STAGE = 6
DEBUG = False

T, C, D, F3, FF = 1024, 4096, 1024, 3072, 16384
TC, CCn, F3C, FCn = 8, 32, 24, 128
FCL = 16
G8 = [list(range(8))]
G4 = [[0, 1, 2, 3], [4, 5, 6, 7]]
EPS = 1e-5
THR = 3.0
INV127 = 1.0 / 127.0
SMIN = 1e-8
ISQ = 1.0 / 32.0


class Ctr:
    def __init__(self, nc, es, name):
        self.sem = es.enter_context(nc.semaphore(name))
        self.v = 0

    def dma(self, inst):
        inst.then_inc(self.sem, 16)
        self.v += 16
        return self.v

    def cc(self, inst):
        inst.then_inc(self.sem)
        self.v += 1
        return self.v

    def inc(self, inst):
        inst.then_inc(self.sem, 1)
        self.v += 1
        return self.v


def build_program(stage=STAGE, debug=DEBUG):
    nc = bass.Bass("TRN2", target_bir_lowering=False)
    es = ExitStack()
    dram = nc.dram_tensor

    x_my = dram("x_my", [T, C], F32, kind="ExternalInput")
    wqkv = dram("wqkv", [F3C, 128, C], F32, kind="ExternalInput")
    wo = dram("wo", [D, C], F32, kind="ExternalInput")
    w1full = dram("w1full", [FCn * 128, C], F32, kind="ExternalInput")
    w2full = dram("w2full", [FF, C], F32, kind="ExternalInput")
    w2scan = dram("w2scan", [FCL * 128, C], F32, kind="ExternalInput")
    bqkv_col = dram("bqkv_col", [128, F3C], F32, kind="ExternalInput")
    b1_col = dram("b1_col", [128, FCn], F32, kind="ExternalInput")
    bo_row = dram("bo_row", [1, C], F32, kind="ExternalInput")
    b2_row = dram("b2_row", [1, C], F32, kind="ExternalInput")
    identbf_d = dram("identbf_d", [128, 128], BF16, kind="ExternalInput")
    identf_d = dram("identf_d", [128, 128], F32, kind="ExternalInput")
    out_my = dram("out_my", [T, C], F32, kind="ExternalOutput")

    xn_sp = dram("xn_sp", [T, C], F32)
    h_d = dram("h_d", [FF, T], F32)
    hq_d = dram("hq_d", [FF, T], BF16)
    ho_d = dram("ho_d", [FF, T], BF16)
    attn_sp = dram("attn_sp", [T, C], F32)
    bnc = dram("bnc", [1, FF], F32)
    xnbf_snap = dram("xnbf_snap", [128, CCn * T], BF16)
    xnbf_snap2 = dram("xnbf_snap2", [128, CCn * T], BF16)

    def ccout(name, shape, dt, kind, groups):
        return dram(name, shape, dt,
                    addr_space=maybe_share_collective_output_space(kind, groups))
    ar0_in = dram("ar0_in", [128, CCn], F32)
    ar0_out = ccout("ar0_out", [128, CCn], F32, "AllReduce", G8)
    ar1_in = dram("ar1_in", [128, CCn], F32)
    ar1_out = ccout("ar1_out", [128, CCn], F32, "AllReduce", G8)
    ar3_in = dram("ar3_in", [128, FCn], F32)
    ar3_out = ccout("ar3_out", [128, FCn], F32, "AllReduce", G8)

    kv_in = dram("kv_in", [2048, D], BF16)
    kv_out = ccout("kv_out", [4 * 2048, D], BF16, "AllGather", G4)
    kvf_out = ccout("kvf_out", [4 * 64, D], BF16, "AllGather", G4)

    dbg = {}
    if debug:
        for name, shape, dt in [
            ("d_xn", [T, C], F32), ("d_cm1", [128, CCn], F32),
            ("d_q", [128, 8 * T], BF16), ("d_kv", [4 * 2048, D], BF16),
            ("d_h", [FF, T], F32), ("d_cmh", [128, FCn], F32),
            ("d_attn", [T, C], F32), ("d_rm2", [128, TC], F32),
            ("d_hq", [FF, T], BF16), ("d_snap", [128, CCn * T], BF16),
            ("d_w1ag", [2 * FCL * 128, C], BF16),
            ("d_w2ag", [2 * FCL * 128, C], BF16), ("d_xq", [128, CCn * T], BF16),
        ]:
            dbg[name] = dram(name, shape, dt, kind="ExternalOutput")

    sQ = Ctr(nc, es, "sQ")
    sG = Ctr(nc, es, "sG")
    sD = Ctr(nc, es, "sD")
    sV = Ctr(nc, es, "sV")
    sA = Ctr(nc, es, "sA")
    sT = Ctr(nc, es, "sT")
    sCC = Ctr(nc, es, "sCC")
    sR = Ctr(nc, es, "sR")
    sGR = Ctr(nc, es, "sGR")
    sFen = Ctr(nc, es, "sFen")
    sKVF = Ctr(nc, es, "sKVF")

    def S(stack, name, shape, dt):
        return stack.enter_context(nc.sbuf_tensor(name, shape, dt))

    def P(stack, name, shape, dt):
        return stack.enter_context(nc.psum_tensor(name, shape, dt))

    ident_bf = S(es, "ident_bf", [128, 128], BF16)
    ident_f = S(es, "ident_f", [128, 128], F32)
    ones_row = S(es, "ones_row", [1, 128], F32)
    bqkv_c = S(es, "bqkv_c", [128, F3C], F32)
    b1_c = S(es, "b1_c", [128, FCn], F32)
    mask1_c = S(es, "mask1_c", [128, CCn], F32)
    mi1_c = S(es, "mi1_c", [128, CCn], F32)
    mask2_c = S(es, "mask2_c", [128, FCn], F32)
    mi2_c = S(es, "mi2_c", [128, FCn], F32)
    cm1loc = S(es, "cm1loc", [128, CCn], F32)
    sw1cols = S(es, "sw1cols", [128, FCL], F32)
    sw2loc = S(es, "sw2loc", [128, CCn], F32)
    cmh = S(es, "cmh", [128, FCn], F32)
    rm2c = S(es, "rm2c", [128, TC], F32)
    swr16 = S(es, "swr16", [32, 128], F32)

    block = es.enter_context(nc.Block())

    def snap():
        return {"V": sV.v, "A": sA.v, "T": sT.v, "Q": sQ.v, "G": sG.v, "D": sD.v}

    def bar_waits(eng, sn, which="VAT"):
        for c, ctr in (("V", sV), ("A", sA), ("T", sT),
                       ("Q", sQ), ("G", sG), ("D", sD)):
            if c in which and sn[c]:
                eng.wait_ge(ctr.sem, sn[c])

    def vsync(vector, inst):
        sV.inc(inst)
        vector.wait_ge(sV.sem, sV.v)

    def asyncr(scalar, inst):
        sA.inc(inst)
        scalar.wait_ge(sA.sem, sA.v)

    def bcastp(src_ap, dst, n, ps, mm_list, ev_list, first_waits=()):
        for i in range(0, n, 512):
            w = min(512, n - i)

            @block.tensor
            def _(tensor, i=i, w=w):
                if i == 0:
                    for sem, val in first_waits:
                        tensor.wait_ge(sem, val)
                if ev_list:
                    tensor.wait_ge(sV.sem, ev_list[-1])
                mm_list.append(sT.inc(tensor.matmul(
                    ps[:, 0:w], ones_row[:, :], src_ap[0:1, i:i + w],
                    start=True, stop=True)))

            @block.vector
            def _(vector, i=i, w=w):
                vector.wait_ge(sT.sem, mm_list[-1])
                ev_list.append(sV.inc(vector.tensor_copy(dst[:, i:i + w],
                                                         ps[:, 0:w])))

    @block.sync
    def _(sync):
        sQ.dma(sync.dma_start(out=ident_bf[:], in_=identbf_d[:, :]))
        sQ.dma(sync.dma_start(out=ident_f[:], in_=identf_d[:, :]))
        sQ.dma(sync.dma_start(out=bqkv_c[:], in_=bqkv_col[:, :]))
        sQ.dma(sync.dma_start(out=b1_c[:], in_=b1_col[:, :]))
    q_consts = sQ.v

    @block.vector
    def _(vector):
        vector.wait_ge(sQ.sem, q_consts)
        sV.inc(vector.memset(ones_row[:], 1.0))

    pw = ExitStack()
    w2b = [S(pw, f"wb{i}", [128, C], F32) for i in range(2)]
    acc2s = S(pw, "acc2s", [128, C], F32)
    ptf2 = [P(pw, f"pwf{i}", [128, 512], F32) for i in range(2)]

    p1 = ExitStack()
    xt = [S(p1, f"p1x{i}", [128, C], F32) for i in range(2)]
    xnt = [S(p1, f"p1n{i}", [128, C], F32) for i in range(2)]
    sqt = S(p1, "p1sq", [128, C], F32)
    cmacc = S(p1, "p1cm", [128, C], F32)
    cst = S(p1, "p1c", [128, 12], F32)
    ptf = [P(p1, f"p1pf{i}", [128, 512], F32) for i in range(2)]

    p1_ld, p1_sq, p1_stats, p1_sqrt, p1_rstd = [], [], [], [], []
    p1_xn, p1_sp, p1_cm = [], [], []

    @block.vector
    def _(vector):
        sV.inc(vector.memset(cmacc[:], 0.0))

    for i in range(TC):
        @block.sync
        def _(sync, i=i):
            if i >= 2:
                sync.wait_ge(sA.sem, p1_xn[i - 2])
            p1_ld.append(sQ.dma(sync.dma_start(
                out=xt[i % 2][:], in_=x_my[i * 128:(i + 1) * 128, :])))

        @block.scalar
        def _(scalar, i=i):
            scalar.wait_ge(sQ.sem, p1_ld[i])
            p1_sq.append(sA.inc(scalar.activation(
                sqt[:], xt[i % 2][:], AF.Square, accum_out=cst[:, 1:2])))

        @block.vector
        def _(vector, i=i):
            vector.wait_ge(sQ.sem, p1_ld[i])
            vsync(vector, vector.reduce_sum(cst[:, 0:1], xt[i % 2][:], AX.X))
            vsync(vector, vector.tensor_scalar(cst[:, 2:3], cst[:, 0:1],
                                               1.0 / C, None, ALU.mult))
            vsync(vector, vector.tensor_mul(cst[:, 4:5], cst[:, 2:3],
                                            cst[:, 2:3]))
            vector.wait_ge(sA.sem, p1_sq[i])
            vsync(vector, vector.tensor_scalar(cst[:, 3:4], cst[:, 1:2],
                                               1.0 / C, None, ALU.mult))
            p1_stats.append(sV.inc(vector.tensor_scalar(
                cst[:, 6:7], cst[:, 3:4], cst[:, 4:5], EPS,
                ALU.subtract, ALU.add)))

        @block.scalar
        def _(scalar, i=i):
            scalar.wait_ge(sV.sem, p1_stats[i])
            p1_sqrt.append(sA.inc(scalar.activation(
                cst[:, 7:8], cst[:, 6:7], AF.Sqrt)))

        @block.vector
        def _(vector, i=i):
            vector.wait_ge(sA.sem, p1_sqrt[i])
            vsync(vector, vector.reciprocal(cst[:, 8:9], cst[:, 7:8]))
            p1_rstd.append(sV.inc(vector.scalar_tensor_tensor(
                cst[:, 9:10], cst[:, 2:3], -1.0, cst[:, 8:9], ALU.mult, ALU.mult)))

        @block.scalar
        def _(scalar, i=i):
            scalar.wait_ge(sV.sem, p1_rstd[i])
            if i >= 2:
                scalar.wait_ge(sD.sem, p1_sp[i - 2])
                scalar.wait_ge(sV.sem, p1_cm[i - 2])
            p1_xn.append(sA.inc(scalar.activation(
                xnt[i % 2][:], xt[i % 2][:], AF.Identity,
                bias=cst[:, 9:10], scale=cst[:, 8:9])))
            scalar.wait_ge(sA.sem, p1_xn[i])
            p1_sp.append(sD.dma(scalar.dma_start(
                out=xn_sp[i * 128:(i + 1) * 128, :], in_=xnt[i % 2][:])))

        @block.vector
        def _(vector, i=i):
            vector.wait_ge(sA.sem, p1_xn[i])
            vector.tensor_tensor(cmacc[:], cmacc[:], xnt[i % 2][:], ALU.max)
            p1_cm.append(sV.inc(vector.scalar_tensor_tensor(
                cmacc[:], xnt[i % 2][:], -1.0, cmacc[:], ALU.mult, ALU.max)))

    p1b_tr, p1b_rd = [], []
    for r in range(CCn):
        @block.tensor
        def _(tensor, r=r):
            if r == 0:
                tensor.wait_ge(sV.sem, p1_cm[-1])
            if r >= 2:
                tensor.wait_ge(sV.sem, p1b_rd[r - 2])
            p1b_tr.append(sT.inc(tensor.transpose(
                ptf[r % 2][:, 0:128], cmacc[:, r * 128:(r + 1) * 128], ident_f[:])))

        @block.vector
        def _(vector, r=r):
            vector.wait_ge(sT.sem, p1b_tr[r])
            p1b_rd.append(sV.inc(vector.reduce_max(
                cm1loc[:, r:r + 1], ptf[r % 2][:, 0:128], AX.X)))

    @block.gpsimd
    def _(gpsimd):
        gpsimd.wait_ge(sV.sem, p1b_rd[-1])
        sG.dma(gpsimd.dma_start(out=ar1_in[:, :], in_=cm1loc[:]))
        gpsimd.wait_ge(sG.sem, sG.v)
        sCC.cc(gpsimd.collective_compute(
            "AllReduce", ALU.max, replica_groups=G8,
            ins=[ar1_in[:, :]], outs=[ar1_out[:, :]]))
    cc_ar1 = sCC.v

    p1_end = snap()
    p1.close()

    p2_ld, p2_ac = [], []

    @block.vector
    def _(vector):
        sV.inc(vector.memset(acc2s[:], 0.0))

    for j in range(FCL):
        @block.sync
        def _(sync, j=j):
            if j >= 2:
                sync.wait_ge(sV.sem, p2_ac[j - 2])
            p2_ld.append(sQ.dma(sync.dma_start(
                out=w2b[j % 2][:], in_=w2scan[j * 128:(j + 1) * 128, :])))

        @block.vector
        def _(vector, j=j):
            vector.wait_ge(sQ.sem, p2_ld[j])
            vector.tensor_tensor(acc2s[:], acc2s[:], w2b[j % 2][:], ALU.max)
            p2_ac.append(sV.inc(vector.scalar_tensor_tensor(
                acc2s[:], w2b[j % 2][:], -1.0, acc2s[:], ALU.mult, ALU.max)))

    p2b_tr, p2b_rd = [], []
    for r in range(CCn):
        @block.tensor
        def _(tensor, r=r):
            if r == 0:
                tensor.wait_ge(sV.sem, p2_ac[-1])
            if r >= 2:
                tensor.wait_ge(sV.sem, p2b_rd[r - 2])
            p2b_tr.append(sT.inc(tensor.transpose(
                ptf2[r % 2][:, 0:128], acc2s[:, r * 128:(r + 1) * 128], ident_f[:])))

        @block.vector
        def _(vector, r=r):
            vector.wait_ge(sT.sem, p2b_tr[r])
            p2b_rd.append(sV.inc(vector.reduce_max(
                sw2loc[:, r:r + 1], ptf2[r % 2][:, 0:128], AX.X)))

    @block.gpsimd
    def _(gpsimd):
        gpsimd.wait_ge(sV.sem, p2b_rd[-1])
        sG.dma(gpsimd.dma_start(out=ar0_in[:, :], in_=sw2loc[:]))
        gpsimd.wait_ge(sG.sem, sG.v)
        sCC.cc(gpsimd.collective_compute(
            "AllReduce", ALU.max, replica_groups=G8,
            ins=[ar0_in[:, :]], outs=[ar0_out[:, :]]))
    cc_ar0 = sCC.v

    @block.gpsimd
    def _(gpsimd):
        gpsimd.wait_ge(sCC.sem, cc_ar1)
        sG.dma(gpsimd.dma_start(out=cm1loc[:], in_=ar1_out[:, :]))
    g_cm1 = sG.v

    @block.vector
    def _(vector):
        vector.wait_ge(sG.sem, g_cm1)
        vsync(vector, vector.tensor_scalar(mask1_c[:], cm1loc[:], THR, None,
                                           ALU.is_gt))
        sV.inc(vector.tensor_scalar(mi1_c[:], mask1_c[:], -1.0, 1.0,
                                    ALU.mult, ALU.add))
    v_mask1 = sV.v

    pw_end = snap()
    pw.close()

    if stage < 2:
        return _finish(nc, block, dbg, debug, locals())

    s_q = ExitStack()
    q_sb = s_q.enter_context(nc.sbuf_tensor("q_sb", [128, 8 * T], BF16,
                                            side="right"))
    s_act = ExitStack()
    xnbf_fm = S(s_act, "xnbf_fm", [128, CCn * T], BF16)
    xq_fm = S(s_act, "xq_fm", [128, CCn * T], BF16)

    p65 = ExitStack()
    xr65 = [S(p65, f"x65_{i}", [128, C], F32) for i in range(2)]
    xb65 = [S(p65, f"b65_{i}", [128, C], BF16) for i in range(2)]
    ptb = [P(p65, f"ptb{i}", [128, 1024], BF16) for i in range(4)]

    p65_ld, p65_c, p65_tr, p65_ev = [], [], [], []
    for i in range(TC):
        @block.sync
        def _(sync, i=i):
            if i == 0:
                bar_waits(sync, pw_end, "VATG")
                sync.wait_ge(sD.sem, p1_sp[-1])
            if i >= 2:
                sync.wait_ge(sA.sem, p65_c[i - 2])
            p65_ld.append(sQ.dma(sync.dma_start(
                out=xr65[i % 2][:], in_=xn_sp[i * 128:(i + 1) * 128, :])))

        @block.scalar
        def _(scalar, i=i):
            scalar.wait_ge(sQ.sem, p65_ld[i])
            if i >= 2:
                scalar.wait_ge(sV.sem, p65_ev[(i - 2) * CCn + CCn - 1])
            p65_c.append(sA.inc(scalar.activation(
                xb65[i % 2][:], xr65[i % 2][:], AF.Identity)))

        for cc in range(CCn):
            @block.tensor
            def _(tensor, i=i, cc=cc):
                n = i * CCn + cc
                if cc == 0:
                    tensor.wait_ge(sA.sem, p65_c[i])
                    if i == 0:
                        bar_waits(tensor, pw_end, "V")
                if n >= 4:
                    tensor.wait_ge(sV.sem, p65_ev[n - 4])
                p65_tr.append(sT.inc(tensor.transpose(
                    ptb[n % 4][:, 0:128],
                    xb65[i % 2][:, cc * 128:(cc + 1) * 128], ident_bf[:])))

            @block.vector
            def _(vector, i=i, cc=cc):
                n = i * CCn + cc
                vector.wait_ge(sT.sem, p65_tr[n])
                p65_ev.append(sV.inc(vector.tensor_copy(
                    xnbf_fm[:, cc * T + i * 128:cc * T + (i + 1) * 128],
                    ptb[n % 4][:, 0:128])))

    @block.gpsimd
    def _(gpsimd):
        gpsimd.wait_ge(sV.sem, p65_ev[-1])
        sG.dma(gpsimd.dma_start(out=xnbf_snap[:, :], in_=xnbf_fm[:]))

    p65_end = snap()
    p65.close()

    if stage < 3:
        return _finish(nc, block, dbg, debug, locals())

    p75 = ExitStack()
    xr75 = S(p75, "x75", [128, C], F32)
    xqb = [S(p75, f"xqb{i}", [128, C], BF16) for i in range(2)]
    mi1b = S(p75, "mi1b", [128, C], BF16)
    row75 = S(p75, "row75", [1, C], F32)
    c75 = S(p75, "c75", [128, 4], F32)
    ptq = [P(p75, f"ptq{i}", [128, 1024], BF16) for i in range(4)]
    pbc2 = P(p75, "pbc2", [128, 512], F32)

    @block.tensor
    def _(tensor):
        bar_waits(tensor, p65_end, "VA")
        tensor.wait_ge(sV.sem, v_mask1)
        sT.inc(tensor.transpose(pbc2[0:CCn, 0:128], mi1_c[:], ident_f[:]))
    t_mi1t = sT.v

    @block.vector
    def _(vector):
        vector.wait_ge(sT.sem, t_mi1t)
        sV.inc(vector.tensor_copy(swr16[0:CCn, 0:128], pbc2[0:CCn, 0:128]))
    v_mi1t = sV.v

    @block.gpsimd
    def _(gpsimd):
        gpsimd.wait_ge(sV.sem, v_mi1t)
        sGR.dma(gpsimd.dma_start(
            out=bnc[0:1, 0:C].rearrange("a (r c) -> (a r) c", r=CCn, c=128),
            in_=swr16[0:CCn, 0:128]))
    g_mi1b = sGR.v

    @block.sync
    def _(sync):
        sync.wait_ge(sGR.sem, g_mi1b)
        sR.dma(sync.dma_start(out=row75[0:1, 0:C], in_=bnc[0:1, 0:C]))
    q_mi1r = sR.v

    bc2_mm, bc2_ev = [], []
    bcastp(row75, mi1b, C, pbc2, bc2_mm, bc2_ev,
           first_waits=((sR.sem, q_mi1r),))

    p75_ld, p75_q, p75_tr, p75_ev = [], [], [], []
    for i in range(TC):
        @block.sync
        def _(sync, i=i):
            if i == 0:
                sync.wait_ge(sV.sem, bc2_ev[-1])
            if i >= 1:
                sync.wait_ge(sV.sem, p75_q[i - 1])
            p75_ld.append(sQ.dma(sync.dma_start(
                out=xr75[:], in_=xn_sp[i * 128:(i + 1) * 128, :])))

        @block.vector
        def _(vector, i=i):
            vector.wait_ge(sQ.sem, p75_ld[i])
            if i >= 2:
                vector.wait_ge(sV.sem, p75_ev[(i - 2) * CCn + CCn - 1])
            vector.tensor_tensor(xr75[:], xr75[:], mi1b[:], ALU.mult)
            vsync(vector, vector.reduce_max(c75[:, 0:1], xr75[:], AX.X,
                                            apply_absolute_value=True))
            vsync(vector, vector.tensor_scalar(c75[:, 1:2], c75[:, 0:1],
                                               INV127, SMIN, ALU.mult, ALU.max))
            vsync(vector, vector.reciprocal(c75[:, 2:3], c75[:, 1:2]))
            vector.tensor_scalar(xr75[:], xr75[:], c75[:, 2:3], MAGIC,
                                 ALU.mult, ALU.add)
            p75_q.append(sV.inc(vector.tensor_scalar(
                xqb[i % 2][:], xr75[:], MAGIC, c75[:, 1:2],
                ALU.subtract, ALU.mult)))

        for cc in range(CCn):
            @block.tensor
            def _(tensor, i=i, cc=cc):
                n = i * CCn + cc
                if cc == 0:
                    tensor.wait_ge(sV.sem, p75_q[i])
                if n >= 4:
                    tensor.wait_ge(sV.sem, p75_ev[n - 4])
                p75_tr.append(sT.inc(tensor.transpose(
                    ptq[n % 4][:, 0:128],
                    xqb[i % 2][:, cc * 128:(cc + 1) * 128], ident_bf[:])))

            @block.vector
            def _(vector, i=i, cc=cc):
                n = i * CCn + cc
                vector.wait_ge(sT.sem, p75_tr[n])
                p75_ev.append(sV.inc(vector.tensor_copy(
                    xq_fm[:, cc * T + i * 128:cc * T + (i + 1) * 128],
                    ptq[n % 4][:, 0:128])))

    @block.gpsimd
    def _(gpsimd):
        gpsimd.wait_ge(sV.sem, p75_ev[-1])
        sG.dma(gpsimd.dma_start(out=xnbf_snap2[:, :], in_=xnbf_fm[:]))

    p75_end = snap()
    p75.close()

    p7 = ExitStack()
    wq7 = [S(p7, f"wq7_{i}", [128, 2048], F32) for i in range(2)]
    wb7 = [S(p7, f"wb7_{i}", [128, C], BF16) for i in range(2)]
    qv7 = [S(p7, f"qv7_{i}", [128, T], BF16) for i in range(2)]
    vt7 = S(p7, "vt7", [128, 8 * T], BF16)
    psQ = [P(p7, f"psQ{i}", [128, 1024], F32) for i in range(3)]
    ptV = P(p7, "ptV", [128, 1024], BF16)

    p7_ld, p7_cast, p7_mm, p7_ev, p7_kst = [], [], [], [], []
    p7_vtr, p7_vev = [], []
    for j in range(F3C):
        for hf in range(2):
            @block.sync
            def _(sync, j=j, hf=hf):
                if j == 0 and hf == 0:
                    bar_waits(sync, p75_end, "VATG")
                n = 2 * j + hf
                if n >= 2:
                    sync.wait_ge(sA.sem, p7_cast[n - 2])
                p7_ld.append(sQ.dma(sync.dma_start(
                    out=wq7[n % 2][:],
                    in_=wqkv[j, :, hf * 2048:(hf + 1) * 2048])))

            @block.scalar
            def _(scalar, j=j, hf=hf):
                n = 2 * j + hf
                scalar.wait_ge(sQ.sem, p7_ld[n])
                if hf == 0 and j >= 2:
                    scalar.wait_ge(sT.sem, p7_mm[j - 2])
                p7_cast.append(sA.inc(scalar.activation(
                    wb7[j % 2][:, hf * 2048:(hf + 1) * 2048],
                    wq7[n % 2][:], AF.Identity)))

        @block.tensor
        def _(tensor, j=j):
            tensor.wait_ge(sA.sem, p7_cast[2 * j + 1])
            if j == 0:
                bar_waits(tensor, p75_end, "V")
            if j >= 3:
                tensor.wait_ge(sA.sem, p7_ev[j - 3])
            ps = psQ[j % 3]
            for th in range(2):
                for cc in range(CCn):
                    mm = tensor.matmul(
                        ps[:, th * 512:(th + 1) * 512],
                        wb7[j % 2][:, cc * 128:(cc + 1) * 128],
                        xnbf_fm[:, cc * T + th * 512:cc * T + th * 512 + 512],
                        start=(cc == 0), stop=(cc == CCn - 1))
            p7_mm.append(sT.inc(mm))

        @block.scalar
        def _(scalar, j=j):
            scalar.wait_ge(sT.sem, p7_mm[j])
            if j < 8:
                dst = q_sb[:, j * T:(j + 1) * T]
            else:
                dst = qv7[j % 2][:]
                if j >= 10:
                    if j - 2 < 16:
                        scalar.wait_ge(sG.sem, p7_kst[j - 2 - 8])
                    else:
                        scalar.wait_ge(sV.sem, p7_vev[(j - 2 - 16) * TC + TC - 1])
            p7_ev.append(sA.inc(scalar.activation(
                dst, psQ[j % 3][:], AF.Identity, bias=bqkv_c[:, j:j + 1])))

        if 8 <= j < 16:
            @block.gpsimd
            def _(gpsimd, j=j):
                gpsimd.wait_ge(sA.sem, p7_ev[j])
                p7_kst.append(sG.dma(gpsimd.dma_start(
                    out=kv_in[(j - 8) * 128:(j - 7) * 128, :],
                    in_=qv7[j % 2][:])))

        if j >= 16:
            for tc in range(TC):
                @block.tensor
                def _(tensor, j=j, tc=tc):
                    n = (j - 16) * TC + tc
                    if tc == 0:
                        tensor.wait_ge(sA.sem, p7_ev[j])
                    if n >= 1:
                        tensor.wait_ge(sV.sem, p7_vev[n - 1])
                    p7_vtr.append(sT.inc(tensor.transpose(
                        ptV[:, 0:128],
                        qv7[j % 2][:, tc * 128:(tc + 1) * 128], ident_bf[:])))

                @block.vector
                def _(vector, j=j, tc=tc):
                    n = (j - 16) * TC + tc
                    vector.wait_ge(sT.sem, p7_vtr[n])
                    p7_vev.append(sV.inc(vector.tensor_copy(
                        vt7[:, tc * T + (j - 16) * 128:tc * T + (j - 15) * 128],
                        ptV[:, 0:128])))

    @block.gpsimd
    def _(gpsimd):
        gpsimd.wait_ge(sV.sem, p7_vev[-1])
        gpsimd.wait_ge(sG.sem, p7_kst[-1])
        sG.dma(gpsimd.dma_start(
            out=bass.AP(kv_in, 1024 * D, [[D, 128], [128 * D, TC], [1, D]]),
            in_=vt7[:]))
        gpsimd.wait_ge(sG.sem, sG.v)
        sCC.cc(gpsimd.collective_compute(
            "AllGather", ALU.bypass, replica_groups=G4,
            ins=[kv_in[:, :]], outs=[kv_out[:, :]]))
        sKVF.cc(gpsimd.collective_compute(
            "AllGather", ALU.bypass, replica_groups=G4,
            ins=[kv_in[0:64, :]], outs=[kvf_out[:, :]]))

    p7_end = snap()
    p7.close()

    if stage < 4:
        return _finish(nc, block, dbg, debug, locals())

    p9 = ExitStack()
    w1f = [S(p9, f"w1f{i}", [128, 2048], F32) for i in range(2)]
    w1qb = [S(p9, f"w1qb{i}", [128, C], BF16) for i in range(2)]
    w1rb = [S(p9, f"w1rb{i}", [128, C], BF16) for i in range(2)]
    rw9 = S(p9, "rw9", [128, 256], BF16)
    c9 = S(p9, "c9", [128, 2], F32)
    row9 = S(p9, "row9", [1, 256], F32)
    hst = [S(p9, f"hst{i}", [128, T], F32) for i in range(2)]
    psF = [P(p9, f"psF{i}", [128, 1024], F32) for i in range(3)]
    ptg = P(p9, "ptg", [128, 1024], BF16)
    pbf = P(p9, "pbf", [128, 512], F32)

    p9_xo = []
    for cc in range(CCn):
        @block.vector
        def _(vector, cc=cc):
            if cc == 0:
                bar_waits(vector, p7_end, "T")
            p9_xo.append(sV.inc(vector.tensor_scalar(
                xnbf_fm[:, cc * T:(cc + 1) * T],
                xnbf_fm[:, cc * T:(cc + 1) * T],
                mask1_c[:, cc:cc + 1], None, ALU.mult)))
    v_xout = sV.v

    p9_ld, p9_mm, p9_ev, p9_hst, p9_cm = [], [], [], [], []
    p9_c, p9_tree, p9_t1, p9_rd, p9_t2, p9_rw, p9_bc, p9_q = \
        [], [], [], [], [], [], [], []
    for g in range(FCn):
        for hf in range(2):
            @block.sync
            def _(sync, g=g, hf=hf):
                if g == 0 and hf == 0:
                    bar_waits(sync, p7_end, "VATG")
                n = 2 * g + hf
                if n >= 2:
                    sync.wait_ge(sA.sem, p9_c[n - 2])
                p9_ld.append(sQ.dma(sync.dma_start(
                    out=w1f[n % 2][:],
                    in_=w1full[g * 128:(g + 1) * 128,
                               hf * 2048:(hf + 1) * 2048])))

            @block.scalar
            def _(scalar, g=g, hf=hf):
                n = 2 * g + hf
                scalar.wait_ge(sQ.sem, p9_ld[n])
                if hf == 0 and g >= 2:
                    scalar.wait_ge(sT.sem, p9_mm[g - 2])
                p9_c.append(sA.inc(scalar.activation(
                    w1rb[g % 2][:, hf * 2048:(hf + 1) * 2048],
                    w1f[n % 2][:], AF.Identity)))

        @block.vector
        def _(vector, g=g):
            vector.wait_ge(sA.sem, p9_c[2 * g + 1])
            b = w1rb[g % 2]
            q = w1qb[g % 2]
            if g >= 2:
                vector.wait_ge(sT.sem, p9_mm[g - 2])
            vector.tensor_tensor(q[:, 0:2048], b[:, 0:2048], b[:, 2048:4096],
                                 ALU.max)
            vector.scalar_tensor_tensor(q[:, 0:2048], b[:, 0:2048], -1.0,
                                        q[:, 0:2048], ALU.mult, ALU.max)
            vector.scalar_tensor_tensor(q[:, 0:2048], b[:, 2048:4096], -1.0,
                                        q[:, 0:2048], ALU.mult, ALU.max)
            vector.tensor_tensor(q[:, 2048:3072], q[:, 0:1024],
                                 q[:, 1024:2048], ALU.max)
            vector.tensor_tensor(q[:, 3072:3584], q[:, 2048:2560],
                                 q[:, 2560:3072], ALU.max)
            vector.tensor_tensor(q[:, 3584:3840], q[:, 3072:3328],
                                 q[:, 3328:3584], ALU.max)
            p9_tree.append(sV.inc(vector.tensor_tensor(
                q[:, 3840:3968], q[:, 3584:3712], q[:, 3712:3840], ALU.max)))

        @block.tensor
        def _(tensor, g=g):
            tensor.wait_ge(sV.sem, p9_tree[g])
            if g >= 1:
                tensor.wait_ge(sV.sem, p9_rd[g - 1])
            p9_t1.append(sT.inc(tensor.transpose(
                ptg[:, 0:128], w1qb[g % 2][:, 3840:3968], ident_bf[:])))

        @block.vector
        def _(vector, g=g):
            vector.wait_ge(sT.sem, p9_t1[g])
            p9_rd.append(sV.inc(vector.reduce_max(
                c9[:, 0:1], ptg[:, 0:128], AX.X)))

        @block.tensor
        def _(tensor, g=g):
            tensor.wait_ge(sV.sem, p9_rd[g])
            if g >= 1:
                tensor.wait_ge(sV.sem, p9_rw[g - 1])
            p9_t2.append(sT.inc(tensor.transpose(
                pbf[0:1, 0:128], c9[:, 0:1], ident_f[:])))

        @block.vector
        def _(vector, g=g):
            vector.wait_ge(sT.sem, p9_t2[g])
            vsync(vector, vector.tensor_copy(row9[0:1, 0:128], pbf[0:1, 0:128]))
            vsync(vector, vector.tensor_scalar(
                row9[0:1, 0:128], row9[0:1, 0:128], INV127, SMIN,
                ALU.mult, ALU.max))
            p9_rw.append(sV.inc(vector.reciprocal(row9[0:1, 128:256],
                                                  row9[0:1, 0:128])))

        @block.tensor
        def _(tensor, g=g):
            tensor.wait_ge(sV.sem, p9_rw[g])
            if g >= 1:
                tensor.wait_ge(sV.sem, p9_bc[g - 1])
            tensor.matmul(pbf[:, 0:128], ones_row[:, :], row9[0:1, 128:256],
                          start=True, stop=True)
            p9_bc.append(sT.inc(tensor.matmul(
                pbf[:, 128:256], ones_row[:, :], row9[0:1, 0:128],
                start=True, stop=True)))

        @block.vector
        def _(vector, g=g):
            vector.wait_ge(sT.sem, p9_bc[g])
            vector.tensor_copy(rw9[:, 0:128], pbf[:, 0:128])
            vector.tensor_copy(rw9[:, 128:256], pbf[:, 128:256])
            b3 = w1rb[g % 2][:].rearrange("p (a b) -> p a b", a=CCn, b=128)
            q3 = w1qb[g % 2][:].rearrange("p (a b) -> p a b", a=CCn, b=128)
            rb_ = bass.AP(rw9, 0, [[256, 128], [0, CCn], [1, 128]])
            sb_ = bass.AP(rw9, 128, [[256, 128], [0, CCn], [1, 128]])
            vector.tensor_tensor(q3, b3, rb_, ALU.mult)
            vector.tensor_scalar(w1qb[g % 2][:], w1qb[g % 2][:], MAGIC, MAGIC,
                                 ALU.add, ALU.subtract)
            p9_q.append(sV.inc(vector.tensor_tensor(q3, q3, sb_, ALU.mult)))

        @block.tensor
        def _(tensor, g=g):
            tensor.wait_ge(sV.sem, p9_q[g])
            if g == 0:
                tensor.wait_ge(sV.sem, v_xout)
                bar_waits(tensor, p7_end, "VA")
            if g >= 3:
                tensor.wait_ge(sA.sem, p9_ev[g - 3])
            ps = psF[g % 3]
            for th in range(2):
                for cc in range(CCn):
                    for v, (blk, act) in enumerate(
                            ((w1qb[g % 2], xq_fm), (w1rb[g % 2], xnbf_fm))):
                        mm = tensor.matmul(
                            ps[:, th * 512:(th + 1) * 512],
                            blk[:, cc * 128:(cc + 1) * 128],
                            act[:, cc * T + th * 512:cc * T + th * 512 + 512],
                            start=(cc == 0 and v == 0),
                            stop=(cc == CCn - 1 and v == 1))
            p9_mm.append(sT.inc(mm))

        @block.scalar
        def _(scalar, g=g):
            scalar.wait_ge(sT.sem, p9_mm[g])
            if g >= 2:
                scalar.wait_ge(sD.sem, p9_hst[g - 2])
                scalar.wait_ge(sV.sem, p9_cm[g - 2])
            p9_ev.append(sA.inc(scalar.activation(
                hst[g % 2][:], psF[g % 3][:], AF.Identity,
                bias=b1_c[:, g:g + 1])))
            scalar.wait_ge(sA.sem, p9_ev[g])
            p9_hst.append(sD.dma(scalar.dma_start(
                out=h_d[g * 128:(g + 1) * 128, :], in_=hst[g % 2][:])))

        @block.vector
        def _(vector, g=g):
            vector.wait_ge(sA.sem, p9_ev[g])
            p9_cm.append(sV.inc(vector.reduce_max(
                cmh[:, g:g + 1], hst[g % 2][:], AX.X,
                apply_absolute_value=True)))

    @block.gpsimd
    def _(gpsimd):
        gpsimd.wait_ge(sV.sem, p9_cm[-1])
        sG.dma(gpsimd.dma_start(out=ar3_in[:, :], in_=cmh[:]))
        gpsimd.wait_ge(sG.sem, sG.v)
        sCC.cc(gpsimd.collective_compute(
            "AllReduce", ALU.max, replica_groups=G8,
            ins=[ar3_in[:, :]], outs=[ar3_out[:, :]]))
    cc_ar3 = sCC.v

    p9_end = snap()
    p9.close()
    s_act.close()

    if stage < 5:
        return _finish(nc, block, dbg, debug, locals())

    p10 = ExitStack()
    avfm = S(p10, "avfm", [128, 8 * T], BF16)
    p10a = ExitStack()
    k_fm = S(p10a, "k_fm", [128, 8 * C], BF16)
    PT = S(p10a, "PT", [128, 32 * 512], BF16)
    sc_sb = [S(p10a, f"sc{i}", [128, C], F32) for i in range(2)]
    Pb = [S(p10a, f"Pb{i}", [128, C], BF16) for i in range(2)]
    vsb = [S(p10a, f"vsb{i}", [128, T], BF16) for i in range(2)]
    c10 = S(p10a, "c10", [128, 8], F32)
    psS = P(p10a, "psS", [128, 1024], F32)
    ptP = [P(p10a, f"ptP{i}", [128, 1024], BF16) for i in range(2)]
    psAV = [P(p10a, f"psAV{i}", [128, 512], F32) for i in range(4)]

    p10_kld = []

    @block.sync
    def _(sync):
        bar_waits(sync, p9_end, "VATD")
        sync.wait_ge(sKVF.sem, 1)
        for dc in range(8):
            for rb in range(4):
                sQ.dma(sync.dma_start(
                    out=k_fm[:, dc * C + rb * T:dc * C + rb * T + T],
                    in_=kv_out[rb * 2048 + dc * 128:rb * 2048 + (dc + 1) * 128, :]))
        p10_kld.append(sQ.v)
    q_kld = sQ.v

    p10_smm = {}
    p10_sev = {}
    p10_sm = {}
    p10_ptr = {}
    p10_pev = {}
    p10_avmm = {}
    p10_avev = {}

    def emit_scores(qc):
        for kq in range(4):
            @block.tensor
            def _(tensor, qc=qc, kq=kq):
                if qc == 0 and kq == 0:
                    tensor.wait_ge(sQ.sem, q_kld)
                    bar_waits(tensor, p9_end, "VA")
                prev = p10_sev.get((qc - (1 if kq == 0 else 0),
                                    (kq - 1) % 4))
                if prev:
                    tensor.wait_ge(sA.sem, prev)
                for d in range(8):
                    for n2 in range(2):
                        mm = tensor.matmul(
                            psS[:, n2 * 512:(n2 + 1) * 512],
                            q_sb[:, d * T + qc * 128:d * T + (qc + 1) * 128],
                            k_fm[:, d * C + kq * 1024 + n2 * 512:
                                 d * C + kq * 1024 + n2 * 512 + 512],
                            start=(d == 0), stop=(d == 7))
                p10_smm[(qc, kq)] = sT.inc(mm)

            @block.scalar
            def _(scalar, qc=qc, kq=kq):
                scalar.wait_ge(sT.sem, p10_smm[(qc, kq)])
                if qc >= 2 and kq == 0:
                    scalar.wait_ge(sV.sem, p10_sm[qc - 2])
                p10_sev[(qc, kq)] = sA.inc(scalar.activation(
                    sc_sb[qc % 2][:, kq * 1024:(kq + 1) * 1024],
                    psS[:], AF.Identity))

        @block.vector
        def _(vector, qc=qc):
            vector.wait_ge(sA.sem, p10_sev[(qc, 3)])
            vsync(vector, vector.reduce_max(c10[:, 0:1], sc_sb[qc % 2][:], AX.X))
            sV.inc(vector.tensor_scalar(c10[:, 1:2], c10[:, 0:1], -ISQ, None,
                                        ALU.mult))
        vmax = sV.v

        @block.scalar
        def _(scalar, qc=qc):
            scalar.wait_ge(sV.sem, vmax)
            if qc >= 2:
                scalar.wait_ge(sT.sem, p10_ptr[(qc - 2, CCn - 1)])
            sA.inc(scalar.activation(Pb[qc % 2][:], sc_sb[qc % 2][:], AF.Exp,
                                     bias=c10[:, 1:2], scale=ISQ,
                                     accum_out=c10[:, 2:3]))
        aexp = sA.v

        @block.vector
        def _(vector, qc=qc):
            vector.wait_ge(sA.sem, aexp)
            vsync(vector, vector.reciprocal(c10[:, 3:4], c10[:, 2:3]))
            p10_sm[qc] = sV.inc(vector.tensor_scalar(
                Pb[qc % 2][:], Pb[qc % 2][:], c10[:, 3:4], None, ALU.mult))

        for kc in range(CCn):
            @block.tensor
            def _(tensor, qc=qc, kc=kc):
                if kc == 0:
                    tensor.wait_ge(sV.sem, p10_sm[qc])
                    if qc == 4:
                        tensor.wait_ge(sT.sem, p10_avmm[1 * CCn + CCn - 1])
                prev = p10_pev.get((qc, kc - 2)) or p10_pev.get((qc - 1, CCn - 2 + kc))
                if prev:
                    tensor.wait_ge(sV.sem, prev)
                p10_ptr[(qc, kc)] = sT.inc(tensor.transpose(
                    ptP[kc % 2][:, 0:128],
                    Pb[qc % 2][:, kc * 128:(kc + 1) * 128], ident_bf[:]))

            @block.vector
            def _(vector, qc=qc, kc=kc):
                vector.wait_ge(sT.sem, p10_ptr[(qc, kc)])
                p10_pev[(qc, kc)] = sV.inc(vector.tensor_copy(
                    PT[:, kc * 512 + (qc % 4) * 128:kc * 512 + (qc % 4) * 128 + 128],
                    ptP[kc % 2][:, 0:128]))

    p10_vld = {}

    def emit_av(half):
        for dh in range(2):
            for kc in range(CCn):
                @block.sync
                def _(sync, half=half, dh=dh, kc=kc):
                    if dh == 0 and kc == 0:
                        sync.wait_ge(sV.sem, p10_pev[(half * 4 + 3, CCn - 1)])
                    n = (half * 2 + dh) * CCn + kc
                    if n >= 2:
                        sync.wait_ge(sT.sem, p10_avmm[n - 2])
                    rb, tl = kc // 8, (kc % 8) * 128
                    p10_vld[n] = sQ.dma(sync.dma_start(
                        out=vsb[kc % 2][:],
                        in_=kv_out[rb * 2048 + 1024 + tl:
                                   rb * 2048 + 1024 + tl + 128, :]))

                @block.tensor
                def _(tensor, half=half, dh=dh, kc=kc):
                    n = (half * 2 + dh) * CCn + kc
                    tensor.wait_ge(sQ.sem, p10_vld[n])
                    if kc == 0 and (half * 2 + dh) >= 1:
                        tensor.wait_ge(sA.sem, p10_avev[half * 2 + dh - 1])
                    for dvl in range(4):
                        dv = dh * 4 + dvl
                        mm = tensor.matmul(
                            psAV[dvl][:, 0:512],
                            vsb[kc % 2][:, dv * 128:(dv + 1) * 128],
                            PT[:, kc * 512:(kc + 1) * 512],
                            start=(kc == 0), stop=(kc == CCn - 1))
                    p10_avmm[n] = sT.inc(mm)

            @block.scalar
            def _(scalar, half=half, dh=dh):
                n = (half * 2 + dh) * CCn + CCn - 1
                scalar.wait_ge(sT.sem, p10_avmm[n])
                for dvl in range(4):
                    dv = dh * 4 + dvl
                    sA.inc(scalar.activation(
                        avfm[:, dv * T + half * 512:dv * T + half * 512 + 512],
                        psAV[dvl][:], AF.Identity))
                p10_avev[half * 2 + dh] = sA.v

    for qc in range(4):
        emit_scores(qc)
    emit_av(0)
    for qc in range(4, 8):
        emit_scores(qc)
    emit_av(1)

    p10a_end = snap()
    p10a.close()

    p10b = ExitStack()
    wof = [S(p10b, f"wof{i}", [128, 1024], F32) for i in range(2)]
    wo_bf = S(p10b, "wo_bf", [128, 8 * C], BF16)
    at_sb = [S(p10b, f"at{i}", [128, C], F32) for i in range(2)]
    xst = [S(p10b, f"xst{i}", [128, C], F32) for i in range(2)]
    bob2b = S(p10b, "bob2b", [128, C], BF16)
    rowB10 = S(p10b, "rowB10", [1, 2048], F32)

    pbias = ExitStack()
    pbc10 = P(pbias, "pbc10", [128, 2048], F32)
    p10_bias_mm, p10_bias_ld = [], []
    for half in range(2):
        @block.sync
        def _(sync, half=half):
            if half == 0:
                bar_waits(sync, p10a_end, "VATD")
            else:
                sync.wait_ge(sT.sem, p10_bias_mm[-1])
            p10_bias_ld.append(sR.dma(sync.dma_start(
                out=rowB10[0:1, :],
                in_=bo_row[0:1, half * 2048:(half + 1) * 2048])))

        @block.tensor
        def _(tensor, half=half):
            tensor.wait_ge(sR.sem, p10_bias_ld[-1])
            if half == 0:
                bar_waits(tensor, p10a_end, "VA")
            for i in range(4):
                mm = tensor.matmul(pbc10[:, i * 512:(i + 1) * 512],
                                   ones_row[:, :],
                                   rowB10[0:1, i * 512:(i + 1) * 512],
                                   start=True, stop=False)
            p10_bias_mm.append(sT.inc(mm))

        @block.sync
        def _(sync, half=half):
            sync.wait_ge(sT.sem, p10_bias_mm[-1])
            p10_bias_ld.append(sR.dma(sync.dma_start(
                out=rowB10[0:1, :],
                in_=b2_row[0:1, half * 2048:(half + 1) * 2048])))

        @block.tensor
        def _(tensor, half=half):
            tensor.wait_ge(sR.sem, p10_bias_ld[-1])
            for i in range(4):
                mm = tensor.matmul(pbc10[:, i * 512:(i + 1) * 512],
                                   ones_row[:, :],
                                   rowB10[0:1, i * 512:(i + 1) * 512],
                                   start=False, stop=True)
            p10_bias_mm.append(sT.inc(mm))

        @block.vector
        def _(vector, half=half):
            vector.wait_ge(sT.sem, p10_bias_mm[-1])
            sV.inc(vector.tensor_copy(bob2b[:, half * 2048:(half + 1) * 2048],
                                      pbc10[:]))
    v_bias10 = sV.v
    pbias.close()

    p10c = ExitStack()
    psP2 = [P(p10c, f"psP{i}", [128, 2048], F32) for i in range(2)]

    p10_wld, p10_wc = [], []
    for dv in range(8):
        for qt in range(4):
            @block.sync
            def _(sync, dv=dv, qt=qt):
                n = dv * 4 + qt
                if n >= 2:
                    sync.wait_ge(sA.sem, p10_wc[n - 2])
                p10_wld.append(sQ.dma(sync.dma_start(
                    out=wof[n % 2][:],
                    in_=wo[dv * 128:(dv + 1) * 128,
                           qt * 1024:(qt + 1) * 1024])))

            @block.scalar
            def _(scalar, dv=dv, qt=qt):
                n = dv * 4 + qt
                scalar.wait_ge(sQ.sem, p10_wld[n])
                p10_wc.append(sA.inc(scalar.activation(
                    wo_bf[:, dv * C + qt * 1024:dv * C + (qt + 1) * 1024],
                    wof[n % 2][:], AF.Identity)))

    p10_pmm, p10_pec, p10_ast, p10_xld, p10_res = {}, {}, [], [], {}
    for qc in range(8):
        @block.sync
        def _(sync, qc=qc):
            if qc >= 2:
                sync.wait_ge(sV.sem, p10_res[qc - 2])
            p10_xld.append(sQ.dma(sync.dma_start(
                out=xst[qc % 2][:], in_=x_my[qc * 128:(qc + 1) * 128, :])))

        for hh in range(2):
            @block.tensor
            def _(tensor, qc=qc, hh=hh):
                n = qc * 2 + hh
                if n == 0:
                    tensor.wait_ge(sA.sem, p10_wc[-1])
                    tensor.wait_ge(sA.sem, p10_avev[3])
                    bar_waits(tensor, p10a_end, "V")
                if n >= 2:
                    tensor.wait_ge(sA.sem, p10_pec[n - 2])
                for dv in range(8):
                    for hg in range(4):
                        mm = tensor.matmul(
                            psP2[n % 2][:, hg * 512:(hg + 1) * 512],
                            avfm[:, dv * T + qc * 128:dv * T + (qc + 1) * 128],
                            wo_bf[:, dv * C + hh * 2048 + hg * 512:
                                  dv * C + hh * 2048 + hg * 512 + 512],
                            start=(dv == 0), stop=(dv == 7))
                p10_pmm[n] = sT.inc(mm)

            @block.scalar
            def _(scalar, qc=qc, hh=hh):
                n = qc * 2 + hh
                scalar.wait_ge(sT.sem, p10_pmm[n])
                if qc >= 2 and hh == 0:
                    scalar.wait_ge(sV.sem, p10_res[qc - 2])
                p10_pec[n] = sA.inc(scalar.activation(
                    at_sb[qc % 2][:, hh * 2048:(hh + 1) * 2048],
                    psP2[n % 2][:], AF.Identity))

        @block.vector
        def _(vector, qc=qc):
            vector.wait_ge(sA.sem, p10_pec[qc * 2 + 1])
            vector.wait_ge(sQ.sem, p10_xld[qc])
            if qc == 0:
                vector.wait_ge(sV.sem, v_bias10)
            if qc >= 2:
                vector.wait_ge(sD.sem, p10_ast[qc - 2])
            a = at_sb[qc % 2][:]
            vector.tensor_add(a, a, xst[qc % 2][:])
            p10_res[qc] = sV.inc(vector.tensor_tensor(a, a, bob2b[:], ALU.add))

        @block.scalar
        def _(scalar, qc=qc):
            scalar.wait_ge(sV.sem, p10_res[qc])
            p10_ast.append(sD.dma(scalar.dma_start(
                out=attn_sp[qc * 128:(qc + 1) * 128, :], in_=at_sb[qc % 2][:])))

    p10c.close()
    p10_end = snap()
    p10b.close()
    p10.close()
    s_q.close()

    if stage < 6:
        return _finish(nc, block, dbg, debug, locals())

    @block.gpsimd
    def _(gpsimd):
        gpsimd.wait_ge(sCC.sem, cc_ar3)
        sG.dma(gpsimd.dma_start(out=cmh[:], in_=ar3_out[:, :]))
    g_cmh = sG.v

    @block.vector
    def _(vector):
        vector.wait_ge(sG.sem, g_cmh)
        vsync(vector, vector.tensor_scalar(mask2_c[:], cmh[:], THR, None,
                                           ALU.is_gt))
        sV.inc(vector.tensor_scalar(mi2_c[:], mask2_c[:], -1.0, 1.0,
                                    ALU.mult, ALU.add))
    v_mask2 = sV.v

    p11 = ExitStack()
    hA = [S(p11, f"hA{i}", [128, T], F32) for i in range(2)]
    hAb = [S(p11, f"hAb{i}", [128, T], F32) for i in range(2)]
    rm2acc = S(p11, "rm2acc", [128, T], F32)
    pr2 = [P(p11, f"pr2_{i}", [128, 512], F32) for i in range(2)]

    @block.vector
    def _(vector):
        bar_waits(vector, p10_end, "AD")
        sV.inc(vector.memset(rm2acc[:], 0.0))

    p11_ld, p11_ab, p11_ac = [], [], []
    for g in range(FCn):
        @block.sync
        def _(sync, g=g):
            if g == 0:
                bar_waits(sync, p10_end, "VATD")
                sync.wait_ge(sD.sem, p9_hst[-1])
            if g >= 2:
                sync.wait_ge(sV.sem, p11_ac[g - 2])
            p11_ld.append(sQ.dma(sync.dma_start(
                out=hA[g % 2][:], in_=h_d[g * 128:(g + 1) * 128, :])))

        @block.scalar
        def _(scalar, g=g):
            scalar.wait_ge(sQ.sem, p11_ld[g])
            if g == 0:
                bar_waits(scalar, p10_end, "VD")
            if g >= 2:
                scalar.wait_ge(sV.sem, p11_ac[g - 2])
            p11_ab.append(sA.inc(scalar.activation(
                hAb[g % 2][:], hA[g % 2][:], AF.Abs)))

        @block.vector
        def _(vector, g=g):
            vector.wait_ge(sA.sem, p11_ab[g])
            if g == 0:
                vector.wait_ge(sV.sem, v_mask2)
            p11_ac.append(sV.inc(vector.scalar_tensor_tensor(
                rm2acc[:], hAb[g % 2][:], mi2_c[:, g:g + 1], rm2acc[:],
                ALU.mult, ALU.max)))

    p11_tr, p11_rd = [], []
    for tcb in range(TC):
        @block.tensor
        def _(tensor, tcb=tcb):
            if tcb == 0:
                tensor.wait_ge(sV.sem, p11_ac[-1])
                bar_waits(tensor, p10_end, "VA")
            if tcb >= 2:
                tensor.wait_ge(sV.sem, p11_rd[tcb - 2])
            p11_tr.append(sT.inc(tensor.transpose(
                pr2[tcb % 2][:, 0:128], rm2acc[:, tcb * 128:(tcb + 1) * 128],
                ident_f[:])))

        @block.vector
        def _(vector, tcb=tcb):
            vector.wait_ge(sT.sem, p11_tr[tcb])
            p11_rd.append(sV.inc(vector.reduce_max(
                rm2c[:, tcb:tcb + 1], pr2[tcb % 2][:, 0:128], AX.X)))

    @block.vector
    def _(vector):
        vector.wait_ge(sV.sem, p11_rd[-1])
        sV.inc(vector.tensor_scalar(rm2c[:], rm2c[:], INV127, SMIN,
                                    ALU.mult, ALU.max))
    v_sx2c = sV.v

    @block.tensor
    def _(tensor):
        tensor.wait_ge(sV.sem, v_sx2c)
        sT.inc(tensor.transpose(pr2[0][0:TC, 0:128], rm2c[:], ident_f[:]))
    t_sx2t = sT.v

    @block.vector
    def _(vector):
        vector.wait_ge(sT.sem, t_sx2t)
        sV.inc(vector.tensor_copy(swr16[0:TC, 0:128], pr2[0][0:TC, 0:128]))
    v_sx2t = sV.v

    @block.gpsimd
    def _(gpsimd):
        gpsimd.wait_ge(sV.sem, v_sx2t)
        sGR.dma(gpsimd.dma_start(
            out=bnc[0:1, 0:T].rearrange("a (r c) -> (a r) c", r=TC, c=128),
            in_=swr16[0:TC, 0:128]))
    g_sx2b = sGR.v

    p11_end = snap()
    p11.close()

    pB = ExitStack()
    rowE = S(pB, "rowE", [1, T], F32)
    rowF = S(pB, "rowF", [1, T], F32)
    sx2b = S(pB, "sx2b", [128, T], F32)
    rsx2b = S(pB, "rsx2b", [128, T], F32)
    hfp = [S(pB, f"hfp{i}", [128, T], F32) for i in range(2)]
    htm = [S(pB, f"htm{i}", [128, T], F32) for i in range(2)]
    hqt = [S(pB, f"hqt{i}", [128, T], BF16) for i in range(2)]
    hot = [S(pB, f"hot{i}", [128, T], BF16) for i in range(2)]
    pbc5 = P(pB, "pbc5", [128, 512], F32)

    @block.sync
    def _(sync):
        bar_waits(sync, p11_end, "VAT")
        sync.wait_ge(sGR.sem, g_sx2b)
        sR.dma(sync.dma_start(out=rowE[0:1, 0:T], in_=bnc[0:1, 0:T]))
    q_sx2r = sR.v

    @block.vector
    def _(vector):
        vector.wait_ge(sR.sem, q_sx2r)
        sV.inc(vector.reciprocal(rowF[0:1, 0:T], rowE[0:1, 0:T]))
    v_sx2r = sV.v

    bc5_mm, bc5_ev = [], []
    bcastp(rowE, sx2b, T, pbc5, bc5_mm, bc5_ev,
           first_waits=((sV.sem, v_sx2r), (sT.sem, p11_end["T"])))
    bcastp(rowF, rsx2b, T, pbc5, bc5_mm, bc5_ev)
    v_sxb = sV.v

    pB_ld, pB_q, pB_o, pB_st = [], [], [], []
    for fc in range(FCn):
        @block.sync
        def _(sync, fc=fc):
            if fc == 0:
                sync.wait_ge(sV.sem, v_sxb)
            if fc >= 2:
                sync.wait_ge(sV.sem, pB_q[fc - 2])
                sync.wait_ge(sA.sem, pB_o[fc - 2])
            sQ.dma(sync.dma_start(out=hfp[fc % 2][:],
                                  in_=h_d[fc * 128:(fc + 1) * 128, :]))
            pB_ld.append(sQ.v)

        @block.vector
        def _(vector, fc=fc):
            vector.wait_ge(sQ.sem, pB_ld[fc])
            if fc >= 2:
                vector.wait_ge(sD.sem, pB_st[fc - 2])
            vector.scalar_tensor_tensor(htm[fc % 2][:], hfp[fc % 2][:],
                                        mi2_c[:, fc:fc + 1], rsx2b[:],
                                        ALU.mult, ALU.mult)
            vector.tensor_scalar(htm[fc % 2][:], htm[fc % 2][:], MAGIC, MAGIC,
                                 ALU.add, ALU.subtract)
            pB_q.append(sV.inc(vector.tensor_tensor(
                hqt[fc % 2][:], htm[fc % 2][:], sx2b[:], ALU.mult)))

        @block.scalar
        def _(scalar, fc=fc):
            scalar.wait_ge(sQ.sem, pB_ld[fc])
            if fc >= 2:
                scalar.wait_ge(sD.sem, pB_st[fc - 2])
            pB_o.append(sA.inc(scalar.activation(
                hot[fc % 2][:], hfp[fc % 2][:], AF.Identity,
                scale=mask2_c[:, fc:fc + 1])))
            scalar.wait_ge(sA.sem, pB_o[fc])
            scalar.wait_ge(sV.sem, pB_q[fc])
            sD.dma(scalar.dma_start(out=hq_d[fc * 128:(fc + 1) * 128, :],
                                    in_=hqt[fc % 2][:]))
            pB_st.append(sD.dma(scalar.dma_start(
                out=ho_d[fc * 128:(fc + 1) * 128, :], in_=hot[fc % 2][:])))

    pB_end = snap()
    pB.close()

    p12 = ExitStack()
    acc = S(p12, "acc", [128, 8 * 1024], F32)
    w2qB = [S(p12, f"w2qB{i}", [128, 8 * 1024], BF16) for i in range(2)]
    w2rB = [S(p12, f"w2rB{i}", [128, 8 * 1024], BF16) for i in range(2)]
    hqB = [S(p12, f"hqB{i}", [128, 8 * 1024], BF16) for i in range(2)]
    hoB = [S(p12, f"hoB{i}", [128, 8 * 1024], BF16) for i in range(2)]
    w2fp = [S(p12, f"w2fp{i}", [128, 1024], F32) for i in range(2)]
    rsw2bf = S(p12, "rsw2bf", [128, C], BF16)
    sw2bf = S(p12, "sw2bf", [128, C], BF16)

    pb6 = ExitStack()
    rowH = S(pb6, "rowH", [1, C], F32)
    pbc6 = P(pb6, "pbc6", [128, C], F32)

    @block.gpsimd
    def _(gpsimd):
        gpsimd.wait_ge(sCC.sem, cc_ar0)
        bar_waits(gpsimd, pB_end, "VATD")
        sG.dma(gpsimd.dma_start(out=sw2loc[:], in_=ar0_out[:, :]))
    g_sw2g = sG.v

    @block.tensor
    def _(tensor):
        tensor.wait_ge(sG.sem, g_sw2g)
        bar_waits(tensor, pB_end, "VA")
        sT.inc(tensor.transpose(pbc6[0:CCn, 0:128], sw2loc[:], ident_f[:]))
    t_sw2t = sT.v

    @block.vector
    def _(vector):
        vector.wait_ge(sT.sem, t_sw2t)
        bar_waits(vector, pB_end, "AD")
        sV.inc(vector.tensor_copy(swr16[0:CCn, 0:128], pbc6[0:CCn, 0:128]))
    v_sw2t = sV.v

    @block.gpsimd
    def _(gpsimd):
        gpsimd.wait_ge(sV.sem, v_sw2t)
        sGR.dma(gpsimd.dma_start(
            out=bnc[0:1, 0:C].rearrange("a (r c) -> (a r) c", r=CCn, c=128),
            in_=swr16[0:CCn, 0:128]))
    g_sw2b = sGR.v

    @block.sync
    def _(sync):
        sync.wait_ge(sGR.sem, g_sw2b)
        sR.dma(sync.dma_start(out=rowH[0:1, 0:C], in_=bnc[0:1, 0:C]))
    q_sw2r = sR.v

    @block.vector
    def _(vector):
        vector.wait_ge(sR.sem, q_sw2r)
        sV.inc(vector.tensor_scalar(rowH[0:1, 0:C], rowH[0:1, 0:C],
                                    INV127, SMIN, ALU.mult, ALU.max))
    v_sw2r = sV.v

    bc6_mm, bc6_ev = [], []
    bcastp(rowH, sw2bf, C, pbc6, bc6_mm, bc6_ev,
           first_waits=((sV.sem, v_sw2r), (sT.sem, t_sw2t)))

    @block.vector
    def _(vector):
        vector.wait_ge(sV.sem, bc6_ev[-1])
        sV.inc(vector.reciprocal(rowH[0:1, 0:C], rowH[0:1, 0:C]))
    v_sw2ri = sV.v

    bcastp(rowH, rsw2bf, C, pbc6, bc6_mm, bc6_ev,
           first_waits=((sV.sem, v_sw2ri),))
    v_swb = sV.v

    pb6.close()

    psF2 = [[P(p12, f"psF2_{s}_{t}", [128, 512], F32) for t in range(4)]
            for s in range(2)]

    NFB = 16
    FPB = 8
    p12_ai = {}
    p12_wld = {}
    p12_wc = []
    p12_wq = []
    p12_mm = {}
    p12_fl = {}
    p12_ost = {}

    for cq in range(4):
        for tc in range(TC):
            @block.sync
            def _(sync, cq=cq, tc=tc):
                if cq == 0 and tc == 0:
                    sync.wait_ge(sD.sem, p10_ast[-1])
                if cq > 0:
                    sync.wait_ge(sG.sem, p12_ost[(cq - 1, tc)])
                p12_ai[(cq, tc)] = sQ.dma(sync.dma_start(
                    out=acc[:, tc * 1024:(tc + 1) * 1024],
                    in_=attn_sp[tc * 128:(tc + 1) * 128,
                                cq * 1024:(cq + 1) * 1024]))

        for fb in range(NFB):
            nfb = cq * NFB + fb

            for fc_in in range(FPB):
                fc = fb * FPB + fc_in

                @block.sync
                def _(sync, cq=cq, fb=fb, nfb=nfb, fc=fc, fc_in=fc_in):
                    if nfb >= 2 and fc_in == 0:
                        sync.wait_ge(sT.sem, p12_mm[(nfb - 2, 3)])
                    if cq == 0 and fc_in == 0:
                        sync.wait_ge(sD.sem, pB_st[fb * FPB + FPB - 1])
                    if nfb == 0 and fc_in == 0:
                        sync.wait_ge(sV.sem, v_swb)
                    n = nfb * FPB + fc_in
                    if n >= 2:
                        sync.wait_ge(sA.sem, p12_wc[n - 2])
                    sQ.dma(sync.dma_start(
                        out=w2fp[n % 2][:],
                        in_=w2full[fc * 128:(fc + 1) * 128,
                                   cq * 1024:(cq + 1) * 1024]))
                    sQ.dma(sync.dma_start(
                        out=hqB[fb % 2][:, fc_in * 1024:(fc_in + 1) * 1024],
                        in_=hq_d[fc * 128:(fc + 1) * 128, :]))
                    sQ.dma(sync.dma_start(
                        out=hoB[fb % 2][:, fc_in * 1024:(fc_in + 1) * 1024],
                        in_=ho_d[fc * 128:(fc + 1) * 128, :]))
                    p12_wld[("fc", nfb, fc_in)] = sQ.v

                @block.scalar
                def _(scalar, cq=cq, fb=fb, nfb=nfb, fc=fc, fc_in=fc_in):
                    n = nfb * FPB + fc_in
                    scalar.wait_ge(sQ.sem, p12_wld[("fc", nfb, fc_in)])
                    if nfb >= 2 and fc_in == 0:
                        scalar.wait_ge(sT.sem, p12_mm[(nfb - 2, 3)])
                    p12_wc.append(sA.inc(scalar.activation(
                        w2rB[fb % 2][:, fc_in * 1024:(fc_in + 1) * 1024],
                        w2fp[n % 2][:], AF.Identity)))

                @block.vector
                def _(vector, cq=cq, fb=fb, nfb=nfb, fc=fc, fc_in=fc_in):
                    n = nfb * FPB + fc_in
                    vector.wait_ge(sA.sem, p12_wc[n])
                    if nfb >= 2 and fc_in == 0:
                        vector.wait_ge(sT.sem, p12_mm[(nfb - 2, 3)])
                    qs = w2qB[fb % 2][:, fc_in * 1024:(fc_in + 1) * 1024]
                    rs = w2rB[fb % 2][:, fc_in * 1024:(fc_in + 1) * 1024]
                    cqs = slice(cq * 1024, (cq + 1) * 1024)
                    vector.tensor_tensor(qs, rs, rsw2bf[:, cqs], ALU.mult)
                    vector.tensor_scalar(qs, qs, MAGIC, MAGIC, ALU.add,
                                         ALU.subtract)
                    p12_wq.append(sV.inc(vector.tensor_tensor(
                        qs, qs, sw2bf[:, cqs], ALU.mult)))

            for grp in range(4):
                tq, ch = grp // 2, grp % 2
                bs = (nfb * 4 + grp) % 2

                @block.tensor
                def _(tensor, cq=cq, fb=fb, nfb=nfb, grp=grp, tq=tq, ch=ch,
                      bs=bs):
                    tensor.wait_ge(sQ.sem, p12_wld[("fc", nfb, FPB - 1)])
                    tensor.wait_ge(sV.sem, p12_wq[nfb * FPB + FPB - 1])
                    ngrp = nfb * 4 + grp
                    if ngrp >= 2:
                        tensor.wait_ge(sV.sem, p12_fl[ngrp - 2])
                    if nfb == 0 and grp == 0:
                        tensor.wait_ge(sV.sem, v_swb)
                    for fc_in in range(FPB):
                        for t in range(4):
                            for v, (hb, wb) in enumerate(
                                    ((hqB[fb % 2], w2qB[fb % 2]),
                                     (hoB[fb % 2], w2rB[fb % 2]))):
                                mm = tensor.matmul(
                                    psF2[bs][t][:, 0:512],
                                    hb[:, fc_in * 1024 + tq * 512 + t * 128:
                                       fc_in * 1024 + tq * 512 + (t + 1) * 128],
                                    wb[:, fc_in * 1024 + ch * 512:
                                       fc_in * 1024 + ch * 512 + 512],
                                    start=(fc_in == 0 and v == 0),
                                    stop=(fc_in == FPB - 1 and v == 1))
                    p12_mm[(nfb, grp)] = sT.inc(mm)

                @block.vector
                def _(vector, cq=cq, fb=fb, nfb=nfb, grp=grp, tq=tq, ch=ch,
                      bs=bs):
                    ngrp = nfb * 4 + grp
                    vector.wait_ge(sT.sem, p12_mm[(nfb, grp)])
                    if fb == 0:
                        for t in range(4):
                            vector.wait_ge(sQ.sem, p12_ai[(cq, tq * 4 + t)])
                    for t in range(4):
                        a = acc[:, (tq * 4 + t) * 1024 + ch * 512:
                                (tq * 4 + t) * 1024 + ch * 512 + 512]
                        fa = vector.tensor_add(a, a, psF2[bs][t][:])
                    p12_fl[ngrp] = sV.inc(fa)

        for tc in range(TC):
            @block.gpsimd
            def _(gpsimd, cq=cq, tc=tc):
                nfb = cq * NFB + NFB - 1
                gpsimd.wait_ge(sV.sem, p12_fl[nfb * 4 + 3])
                p12_ost[(cq, tc)] = sG.dma(gpsimd.dma_start(
                    out=out_my[tc * 128:(tc + 1) * 128,
                               cq * 1024:(cq + 1) * 1024],
                    in_=acc[:, tc * 1024:(tc + 1) * 1024]))

    return _finish(nc, block, dbg, debug, locals())


def _finish(nc, block, dbg, debug, env):
    sG, sD, sQ, sCC = env["sG"], env["sD"], env["sQ"], env["sCC"]
    sV, sA, sT = env["sV"], env["sA"], env["sT"]
    sR, sGR = env["sR"], env["sGR"]
    extra = [env.get("sFen"), env.get("sKVF")]

    if debug:
        pairs = []
        sbp = []
        t = env.get("xq_fm")
        if t is not None and "d_xq" in dbg and env.get("stage_sb_dump", True):
            try:
                sbp.append((dbg["d_xq"], t))
            except Exception:
                pass
        for name, src in [
            ("d_xn", env.get("xn_sp")), ("d_h", env.get("h_d")),
            ("d_attn", env.get("attn_sp")), ("d_hq", env.get("hq_d")),
            ("d_kv", env.get("kv_out")), ("d_w1ag", None),
            ("d_w2ag", None), ("d_cm1", env.get("ar1_out")),
            ("d_cmh", env.get("ar3_out")), ("d_snap", env.get("xnbf_snap")),
            ("d_attn", env.get("xnbf_snap2")) if env.get("stage", 9) < 5 else ("d_snap", env.get("xnbf_snap")),
        ]:
            if name in dbg and src is not None:
                pairs.append((dbg[name], src))

        @block.gpsimd
        def _(gpsimd):
            if sV.v:
                gpsimd.wait_ge(sV.sem, sV.v)
            if sA.v:
                gpsimd.wait_ge(sA.sem, sA.v)
            if sT.v:
                gpsimd.wait_ge(sT.sem, sT.v)
            if sD.v:
                gpsimd.wait_ge(sD.sem, sD.v)
            if sCC.v:
                gpsimd.wait_ge(sCC.sem, sCC.v)
            for d, s in pairs:
                sG.dma(gpsimd.dma_start(out=d[:, :], in_=s[:, :]))
            for d, s in sbp:
                sG.dma(gpsimd.dma_start(out=d[:, :], in_=s[:]))

    @block.gpsimd
    def _(gpsimd):
        for s_ in extra:
            if s_ is not None and s_.v:
                gpsimd.wait_ge(s_.sem, s_.v)
        if sR.v:
            gpsimd.wait_ge(sR.sem, sR.v)
        if sGR.v:
            gpsimd.wait_ge(sGR.sem, sGR.v)
        if sQ.v:
            gpsimd.wait_ge(sQ.sem, sQ.v)
        if sD.v:
            gpsimd.wait_ge(sD.sem, sD.v)
        if sCC.v:
            gpsimd.wait_ge(sCC.sem, sCC.v)
        if sV.v:
            gpsimd.wait_ge(sV.sem, sV.v)
        if sA.v:
            gpsimd.wait_ge(sA.sem, sA.v)
        if sT.v:
            gpsimd.wait_ge(sT.sem, sT.v)
        gpsimd.wait_ge(sG.sem, sG.v)
    return nc


def _host_layout(inputs):
    x = np.ascontiguousarray(
        np.asarray(inputs["L0_attn_x"], dtype=np.float32).reshape(8 * T, C))
    Wqkv = np.asarray(inputs["L0_attn_Wqkv"], dtype=np.float32)
    bqkv = np.asarray(inputs["L0_attn_bqkv"], dtype=np.float32)
    Wo = np.ascontiguousarray(np.asarray(inputs["L0_attn_Wo"], dtype=np.float32))
    bo = np.asarray(inputs["L0_attn_bo"], dtype=np.float32)
    W1 = np.asarray(inputs["L0_ffn_W1"], dtype=np.float32)
    b1 = np.asarray(inputs["L0_ffn_b1"], dtype=np.float32)
    W2 = np.asarray(inputs["L0_ffn_W2"], dtype=np.float32)
    b2 = np.asarray(inputs["L0_ffn_b2"], dtype=np.float32)

    wqkv_r = np.ascontiguousarray(
        Wqkv.reshape(CCn, 128, F3C, 128).transpose(2, 1, 0, 3)
    ).reshape(F3C, 128, C)
    bqkv_col = np.ascontiguousarray(bqkv.reshape(F3C, 128).T)
    b1_col = np.ascontiguousarray(b1.reshape(FCn, 128).T)
    bo_row = bo.reshape(1, C)
    b2_row = b2.reshape(1, C)
    identbf = np.eye(128, dtype=ml_dtypes.bfloat16)
    identf = np.eye(128, dtype=np.float32)

    w1full = np.ascontiguousarray(
        W1.reshape(FCn, 128, CCn, 128).transpose(0, 3, 2, 1)).reshape(FCn * 128, C)
    w2full = np.ascontiguousarray(W2.T)
    maps = []
    for c in range(8):
        maps.append({
            "x_my": np.ascontiguousarray(x[c * T:(c + 1) * T]),
            "wqkv": wqkv_r,
            "wo": Wo,
            "w1full": w1full,
            "w2full": w2full,
            "w2scan": w2full[c * FCL * 128:(c + 1) * FCL * 128],
            "bqkv_col": bqkv_col,
            "b1_col": b1_col,
            "bo_row": bo_row,
            "b2_row": b2_row,
            "identbf_d": identbf,
            "identf_d": identf,
        })
    return maps


def kernel(**inputs):
    maps = _host_layout(inputs)
    nc = build_program()
    res = run_bass_kernel_spmd(nc, maps, core_ids=list(range(8)))
    out = np.concatenate([np.asarray(res.results[i]["out_my"])
                          for i in range(8)], axis=0)
    return out.reshape(2, 4096, C)
